# version 11
# speedup vs baseline: 1.0594x; 1.0065x over previous
import contextlib

import numpy as np

import concourse.bass as bass
import concourse.mybir as mybir
from concourse.bass_utils import run_bass_kernel_spmd

F32 = mybir.dt.float32
BF16 = mybir.dt.bfloat16
I16 = mybir.dt.int16
ALU = mybir.AluOpType
AF = mybir.ActivationFunctionType
AP = bass.AP
AX = mybir.AxisListType

T = 2048
TT = 512
NT = T // TT
D = 1024
DC = D // 128
I0 = 3072
I0C = I0 // 128
F1 = 4096
F1C = F1 // 128
NHEAD = 16
DH = 64
CHUNK = 128
NCH = T // CHUNK

_UPTO = 6
_DBG = None


class P:

    def __init__(self, nc, es):
        self.nc = nc
        self.es = es
        self.cnt = {}
        self.sems = {}
        for s in ["sS", "sSa", "sSb", "sSc", "sSd", "sSe", "sSf", "sSg", "sSh",
                  "sG", "sGa", "sGb", "sGc", "sGd", "sC", "sPE", "sDV", "sAC", "sCC"]:
            self.sems[s] = es.enter_context(nc.semaphore(s))
            self.cnt[s] = 0
        self.dma_sems = ["sS", "sSa", "sSb", "sSc", "sSd", "sSe", "sSf", "sSg", "sSh",
                        "sG", "sGa", "sGb", "sGc", "sGd"]

    def inc(self, instr, name, n=1):
        instr.then_inc(self.sems[name], n)
        self.cnt[name] += n
        return self.cnt[name]

    def w(self, eng, name, val=None):
        eng.wait_ge(self.sems[name], self.cnt[name] if val is None else val)

    def fence(self, eng, snap=None):
        c = self.cnt if snap is None else snap
        for name in self.dma_sems:
            if c[name] > 0:
                eng.wait_ge(self.sems[name], c[name])

    def snap(self):
        return dict(self.cnt)


def build(upto=6, dbg=None, race=False):
    nc = bass.Bass("TRN2", target_bir_lowering=False, detect_race_conditions=race)
    es = contextlib.ExitStack()
    nc._es = es
    p = P(nc, es)

    d_x = nc.dram_tensor("x_t", [D, T], F32, kind="ExternalInput")
    d_k0 = [nc.dram_tensor(f"k0_{i}", [D, D], F32, kind="ExternalInput") for i in range(3)]
    d_wo0 = nc.dram_tensor("wo0", [D, D], F32, kind="ExternalInput")
    d_wgate = nc.dram_tensor("w_gate", [I0, D], F32, kind="ExternalInput")
    d_wup = nc.dram_tensor("w_up", [I0, D], F32, kind="ExternalInput")
    d_wdown = nc.dram_tensor("w_down", [D, I0], F32, kind="ExternalInput")
    d_wq = nc.dram_tensor("wqT", [D, D], F32, kind="ExternalInput")
    d_wk = nc.dram_tensor("wkT", [D, D], F32, kind="ExternalInput")
    d_wv = nc.dram_tensor("wvT", [D, D], F32, kind="ExternalInput")
    d_wo1 = nc.dram_tensor("woT", [D, D], F32, kind="ExternalInput")
    d_battn = nc.dram_tensor("battn4", [4, D], F32, kind="ExternalInput")
    d_w1 = nc.dram_tensor("w1", [F1, D], F32, kind="ExternalInput")
    d_b1 = nc.dram_tensor("b1", [F1], F32, kind="ExternalInput")
    d_w2 = nc.dram_tensor("w2", [D, F1], F32, kind="ExternalInput")
    d_b2 = nc.dram_tensor("b2", [D], F32, kind="ExternalInput")
    d_out = nc.dram_tensor("out_t", [D, T], F32, kind="ExternalOutput")

    s_stage_ud = nc.dram_tensor("s_stage_ud", [I0, D], BF16)
    s_stage_dn = nc.dram_tensor("s_stage_dn", [D, I0], BF16)
    s_stage_w1 = nc.dram_tensor("s_stage_w1", [F1, D], BF16)
    s_stage_w2 = nc.dram_tensor("s_stage_w2", [D, F1], BF16)
    sp_g2 = nc.dram_tensor("sp_g2", [I0, T], BF16)
    sp_o = nc.dram_tensor("sp_o", [I0, T], BF16)
    s_gateT = nc.dram_tensor("s_gateT", [D, I0], BF16)
    s_upT = nc.dram_tensor("s_upT", [D, I0], BF16)
    s_downT = nc.dram_tensor("s_downT", [I0, D], BF16)
    s_w1T = nc.dram_tensor("s_w1T", [D, F1], BF16)
    s_w2T = nc.dram_tensor("s_w2T", [F1, D], BF16)
    sp_up = nc.dram_tensor("sp_up", [I0, T], BF16)
    sp_gate = nc.dram_tensor("sp_gate", [I0, T], BF16)
    sp_h = nc.dram_tensor("sp_h", [F1, T], BF16)
    d_dend = nc.dram_tensor("d_dend", [NHEAD, T], F32)
    d_rdend = nc.dram_tensor("d_rdend", [NHEAD, T], F32)
    d_red = nc.dram_tensor("d_red", [128], F32)
    d_cin = nc.dram_tensor("d_cin", [8], F32)
    d_cout = nc.dram_tensor("d_cout", [8], F32)

    sb = lambda name, shape, dt: es.enter_context(nc.sbuf_tensor(name, shape, dt))
    x_res = [sb(f"xres{c}", [128, T], F32) for c in range(DC)]
    xln = [sb(f"xln{c}", [128, T], BF16) for c in range(DC)]
    abuf = sb("abuf", [128, 20736], BF16)
    acc2 = sb("acc2", [128, 16384], BF16)
    tmpa = sb("tmpa", [128, 1024], F32)
    tmpb = sb("tmpb", [128, 1024], F32)
    wst = [sb(f"wst{i}", [128, 4096], BF16) for i in range(2)]
    stg = [acc2[:, 0:T], acc2[:, T:2 * T]]
    iti = [sb(f"iti{i}", [128, TT], I16) for i in range(2)]
    iti2 = [sb(f"iti2{i}", [128, TT], I16) for i in range(2)]
    sq2 = sb("sq2", [128, 2 * TT], BF16)
    sqt = [sq2[:, 0:TT], sq2[:, TT:2 * TT]]
    prepi = sb("prepi", [128, 1024], I16)
    preps = sb("preps", [128, 64], F32)
    onesd = sb("onesd", [128, 128], BF16)
    ones1 = sb("ones1", [128, 1], BF16)
    onesr = sb("onesr", [1, 128], BF16)
    trimask = sb("trimask", [128, 128], BF16)
    ident = sb("ident", [128, 128], BF16)
    maxac = sb("maxac", [128, 8], F32)
    mxc = sb("mxc", [128, 32], F32)
    rowb = sb("rowb", [1, 128], F32)
    locs = sb("locs", [1, 8], F32)
    gbc = sb("gbc", [128, 8], F32)
    scl = sb("scl", [128, 16], F32)
    sw1 = sb("sw1", [128, F1C], F32)
    sw2 = sb("sw2", [128, DC], F32)
    b1t = sb("b1t", [128, F1C], F32)
    b2t = sb("b2t", [128, DC], F32)
    b1q = sb("b1q", [128, F1C], F32)
    b2q = sb("b2q", [128, DC], F32)
    battn = sb("battn", [128, 4 * DC], F32)
    bkbc = sb("bkbc", [128, D], BF16)
    bvbc = sb("bvbc", [128, D], BF16)
    zex = sb("zex", [128, DC], F32)
    zexb = sb("zexb", [128, DC], BF16)
    constt = sb("constt", [128, 8], F32)

    ps = [es.enter_context(nc.psum_tensor(f"ps{i}", [128, TT], F32)) for i in range(4)]
    psW = es.enter_context(nc.psum_tensor("psW", [128, 2 * TT], F32))
    ps.append(psW[:, 0:TT])
    ps.append(psW[:, TT:2 * TT])
    psm = es.enter_context(nc.psum_tensor("psm", [128, 128], F32))
    pst = es.enter_context(nc.psum_tensor("pst", [128, 128], BF16))

    ctx = dict(locals())

    with nc.Block() as block:
        @block.sync
        def _(sync):
            for c in range(DC):
                p.inc(sync.dma_start(out=x_res[c][:, :], in_=d_x[c * 128:(c + 1) * 128, :]), "sS", 16)
            for j in range(4):
                p.inc(sync.dma_start(
                    out=AP(battn, j * DC, [[4 * DC, 128], [1, DC]]),
                    in_=AP(d_battn, j * D, [[1, 128], [128, DC]]),
                    allow_slow_non_contiguous=True), "sS", 16)
            p.inc(sync.dma_start(out=AP(b1t, 0, [[F1C, 128], [1, F1C]]),
                                 in_=AP(d_b1, 0, [[1, 128], [128, F1C]]),
                                 allow_slow_non_contiguous=True), "sS", 16)
            p.inc(sync.dma_start(out=AP(b2t, 0, [[DC, 128], [1, DC]]),
                                 in_=AP(d_b2, 0, [[1, 128], [128, DC]]),
                                 allow_slow_non_contiguous=True), "sS", 16)
            p.w(sync, "sS")

        @block.gpsimd
        def _(g):
            p.inc(g.dma_start(out=bkbc[:, :], in_=AP(d_battn, D, [[0, 128], [1, D]])), "sG", 16)
            p.inc(g.dma_start(out=bvbc[:, :], in_=AP(d_battn, 2 * D, [[0, 128], [1, D]])), "sG", 16)
            g.memset(constt[:, 0:1], 1e-5)
            g.memset(constt[:, 1:2], 1e-6)
            g.memset(constt[:, 2:3], 1.0 / 65536.0)
            g.memset(onesd[:, :], 1.0 / 1024.0)
            g.memset(ones1[:, :], 1.0)
            g.memset(onesr[:, :], 1.0)
            i = g.memset(trimask[:, :], 1.0)
            p.w(g, "sCC", p.inc(i, "sCC"))
            g.affine_select(out=trimask[:, :], in_=trimask[:, :], pattern=[[1, 128]],
                            compare_op=ALU.is_ge, fill=0.0, base=0, channel_multiplier=-1)
            i = g.memset(ident[:, :], 1.0)
            p.w(g, "sCC", p.inc(i, "sCC"))
            g.affine_select(out=ident[:, :], in_=ident[:, :], pattern=[[-1, 128]],
                            compare_op=ALU.is_equal, fill=0.0, base=0, channel_multiplier=1)
            p.w(g, "sG")

        @block.vector
        def _(v):
            v.memset(maxac[:, :], 0.0)
            v.memset(zex[:, :], 0.0)
            v.memset(locs[:, :], 0.0)

    if upto > 0:
        _prep_l0(ctx, d_wup, s_stage_ud, s_upT, I0, D)
        _prep_l0(ctx, d_wgate, s_stage_ud, s_gateT, I0, D)
        _prep_l0(ctx, d_wdown, s_stage_dn, s_downT, D, I0)
    if upto > 3:
        _prep_l1(ctx, d_w1, s_stage_w1, s_w1T, F1, D, sw1)
        _prep_l1(ctx, d_w2, s_stage_w2, s_w2T, D, F1, sw2)

    if dbg == "xln":
        _layernorm(ctx)
    elif dbg == "l1h":
        _ffn_l0(ctx, sp_up, sp_gate)
        _attn_l0(ctx)
        _ffn_l0(ctx, sp_up, sp_gate)
        _ffn_l1(ctx, stop_after_h=True)
    else:
        if upto > 0:
            _ffn_l0(ctx, sp_up, sp_gate)
        if upto > 1:
            _attn_l0(ctx)
        if upto > 2:
            _ffn_l0(ctx, sp_up, sp_gate)
        if upto > 3:
            _ffn_l1(ctx)
        if upto > 4:
            _attn_l1(ctx)
        if upto > 5:
            _ffn_l1(ctx)

    src = xln if dbg == "xln" else x_res
    with nc.Block() as block:
        @block.gpsimd
        def _(g):
            if dbg == "l1h":
                p.inc(g.dma_start(out=d_out[:, :], in_=sp_h[0:1024, :]), "sG", 16)
            else:
                for c in range(DC):
                    p.inc(g.dma_start(out=d_out[c * 128:(c + 1) * 128, :], in_=src[c][:, :]), "sG", 16)
            g.wait_ge(p.sems["sG"], p.cnt["sG"])
    return nc


def _bcast_stats(ctx, src_of, func):
    p, nc = ctx["p"], ctx["nc"]
    sqt, onesd, ps = ctx["sqt"], ctx["onesd"], ctx["ps"]
    with nc.Block() as block:
        b_ac = p.cnt["sAC"]
        b_pe = p.cnt["sPE"]

        @block.scalar
        def _(s):
            for tt in range(NT):
                for c in range(DC):
                    i = tt * DC + c
                    if i >= 2:
                        p.w(s, "sPE", b_pe + i - 1)
                    ins = s.activation(out=sqt[i % 2][:, :], in_=src_of(c, tt), func=func)
                    p.inc(ins, "sAC")

        @block.tensor
        def _(t):
            for tt in range(NT):
                for c in range(DC):
                    i = tt * DC + c
                    p.w(t, "sAC", b_ac + i + 1)
                    ins = t.matmul(ps[tt][:, :], onesd[:, :], sqt[i % 2][:, :],
                                   start=(c == 0), stop=(c == DC - 1))
                    p.inc(ins, "sPE")


def _layernorm(ctx):
    p, nc = ctx["p"], ctx["nc"]
    x_res, xln, tmpa, tmpb = ctx["x_res"], ctx["xln"], ctx["tmpa"], ctx["tmpb"]
    ps = ctx["ps"]

    _bcast_stats(ctx, lambda c, tt: x_res[c][:, tt * TT:(tt + 1) * TT], AF.Copy)

    with nc.Block() as block:
        @block.vector
        def _(v):
            for tt in range(NT):
                for c in range(DC):
                    v.tensor_sub(out=xln[c][:, tt * TT:(tt + 1) * TT],
                                 in0=x_res[c][:, tt * TT:(tt + 1) * TT], in1=ps[tt][:, :])

    _bcast_stats(ctx, lambda c, tt: xln[c][:, tt * TT:(tt + 1) * TT], AF.Square)

    with nc.Block() as block:
        b_ac = p.cnt["sAC"]
        b_dv = p.cnt["sDV"]

        @block.scalar
        def _(s):
            for tt in range(NT):
                if tt >= 2:
                    p.w(s, "sDV", b_dv + 2 * tt - 3)
                ins = s.activation(out=tmpa[:, (tt % 2) * TT:(tt % 2 + 1) * TT], in_=ps[tt][:, :],
                                   func=AF.Sqrt, bias=ctx["constt"][:, 0:1])
                p.inc(ins, "sAC")

        @block.vector
        def _(v):
            for tt in range(NT):
                p.w(v, "sDV")
                p.w(v, "sAC", b_ac + tt + 1)
                ins = v.reciprocal(out=tmpb[:, (tt % 2) * TT:(tt % 2 + 1) * TT],
                                   in_=tmpa[:, (tt % 2) * TT:(tt % 2 + 1) * TT])
                cdv = p.inc(ins, "sDV")
                p.w(v, "sDV", cdv)
                for c in range(DC):
                    ins = v.tensor_mul(out=xln[c][:, tt * TT:(tt + 1) * TT],
                                       in0=xln[c][:, tt * TT:(tt + 1) * TT],
                                       in1=tmpb[:, (tt % 2) * TT:(tt % 2 + 1) * TT])
                    if c == DC - 1:
                        p.inc(ins, "sDV")


def _transpose_bounce(ctx, s_stage, s_T, odim, idim):
    p, nc = ctx["p"], ctx["nc"]
    wst, abuf = ctx["wst"], ctx["abuf"]
    slots = [wst[0][:, 0:odim], wst[1][:, 0:odim],
             abuf[:, 0:odim], abuf[:, 4096:4096 + odim]]
    ssem = ["sSa", "sSb", "sSc", "sSd"]
    gsem = ["sGa", "sGb", "sGc", "sGd"]
    snap = p.snap()
    with nc.Block() as block:
        b_sp = [p.cnt[s] for s in ssem]
        b_gp = [p.cnt[s] for s in gsem]
        ic = idim // 128

        def load(eng, t):
            if t >= 4:
                p.w(eng, gsem[t % 4], b_gp[t % 4] + 16 * (t // 4))
            p.inc(eng.dma_start(out=slots[t % 4],
                                in_=s_stage[0:odim, t * 128:(t + 1) * 128], transpose=True),
                  ssem[t % 4], 16)

        @block.sync
        def _(sync):
            p.fence(sync, snap)
            for t in range(ic):
                load(sync, t)

        @block.gpsimd
        def _(g):
            for t in range(ic):
                p.w(g, ssem[t % 4], b_sp[t % 4] + 16 * (t // 4 + 1))
                p.inc(g.dma_start(out=s_T[t * 128:(t + 1) * 128, 0:odim], in_=slots[t % 4]),
                      gsem[t % 4], 16)


def _prep_l0(ctx, d_w, s_stage, s_T, odim, idim):
    p, nc = ctx["p"], ctx["nc"]
    tmpa, tmpb, prepi, preps = ctx["tmpa"], ctx["tmpb"], ctx["prepi"], ctx["preps"]
    stg = ctx["stg"]
    oc, ncc = odim // 128, idim // 1024
    fb = [tmpa, tmpb]
    ob = [stg[0][:, 0:1024], stg[0][:, 1024:2048]]

    with nc.Block() as block:
        b_sp = [p.cnt["sSa"], p.cnt["sSb"]]
        b_gp = [p.cnt["sGa"], p.cnt["sGb"]]
        b_dv = p.cnt["sDV"]
        NV = 5

        @block.sync
        def _(sync):
            p.fence(sync)
            for i in range(oc * ncc):
                t, cc = divmod(i, ncc)
                if i >= 2:
                    p.w(sync, "sDV", b_dv + NV * (i - 2) + 4)
                p.inc(sync.dma_start(out=fb[i % 2][:, :],
                                     in_=d_w[t * 128:(t + 1) * 128, cc * 1024:(cc + 1) * 1024]),
                      "sSa" if i % 2 == 0 else "sSb", 16)

        @block.vector
        def _(v):
            for i in range(oc * ncc):
                p.w(v, "sDV")
                p.w(v, "sSa" if i % 2 == 0 else "sSb", b_sp[i % 2] + 16 * (i // 2 + 1))
                if i >= 2:
                    p.w(v, "sGa" if i % 2 == 0 else "sGb", b_gp[i % 2] + 16 * (i // 2))
                f = fb[i % 2]
                i0 = v.tensor_reduce(out=preps[:, 0:32], in_=AP(f, 0, [[1024, 128], [32, 32], [1, 32]]),
                                     axis=AX.X, op=ALU.max, apply_absolute_value=True)
                p.w(v, "sDV", p.inc(i0, "sDV"))
                i1 = v.tensor_scalar(out=preps[:, 0:32], in0=preps[:, 0:32],
                                     scalar1=1.0 / 7.0, scalar2=1e-12, op0=ALU.mult, op1=ALU.max)
                p.w(v, "sDV", p.inc(i1, "sDV"))
                i2 = v.reciprocal(out=preps[:, 32:64], in_=preps[:, 0:32])
                p.w(v, "sDV", p.inc(i2, "sDV"))
                i3 = v.tensor_tensor(out=prepi[:, :], in0=f[:, :],
                                     in1=AP(preps, 32, [[64, 128], [1, 32], [0, 32]]), op=ALU.mult)
                p.w(v, "sDV", p.inc(i3, "sDV"))
                i4 = v.scalar_tensor_tensor(out=ob[i % 2], in0=prepi[:, :], scalar=1.0,
                                            in1=AP(preps, 0, [[64, 128], [1, 32], [0, 32]]),
                                            op0=ALU.mult, op1=ALU.mult)
                p.inc(i4, "sDV")

        @block.gpsimd
        def _(g):
            for i in range(oc * ncc):
                t, cc = divmod(i, ncc)
                p.w(g, "sDV", b_dv + NV * (i + 1))
                p.inc(g.dma_start(out=s_stage[t * 128:(t + 1) * 128, cc * 1024:(cc + 1) * 1024],
                                  in_=ob[i % 2]), "sGa" if i % 2 == 0 else "sGb", 16)

        @block.tensor
        def _(t):
            od, pm = ctx["onesd"], ctx["psm"]
            for i in range(oc * ncc):
                for d in range(2):
                    p.w(t, "sDV", b_dv + NV * i + 2 * d + 2)
                    for _r in range(4):
                        t.matmul(pm[:, :], od[:, :], od[:, :], start=True, stop=True)

    _transpose_bounce(ctx, s_stage, s_T, odim, idim)


def _prep_l1(ctx, d_w, s_stage, s_T, odim, idim, swout):
    p, nc = ctx["p"], ctx["nc"]
    tmpa, tmpb, prepi, preps = ctx["tmpa"], ctx["tmpb"], ctx["prepi"], ctx["preps"]
    stg = ctx["stg"]
    oc, ncc = odim // 128, idim // 1024
    fb = [tmpa, tmpb]
    ob = [stg[0][:, 0:1024], stg[0][:, 1024:2048]]

    with nc.Block() as block:
        b_sp = [p.cnt["sSa"], p.cnt["sSb"]]
        b_gp = [p.cnt["sGa"], p.cnt["sGb"]]
        b_dv = p.cnt["sDV"]
        NVT = 3 * ncc + 3

        def wload(eng, n_load):
            i = n_load - 1
            p.w(eng, "sSa" if i % 2 == 0 else "sSb", b_sp[i % 2] + 16 * (i // 2 + 1))

        @block.sync
        def _(sync):
            p.fence(sync)
            n_load = 0
            for t in range(oc):
                for cc in range(ncc):
                    if t > 0 or cc >= 2:
                        if cc < 2:
                            p.w(sync, "sDV", b_dv + NVT * t)
                        else:
                            p.w(sync, "sDV", b_dv + NVT * t + cc - 1)
                    p.inc(sync.dma_start(out=fb[n_load % 2][:, :],
                                         in_=d_w[t * 128:(t + 1) * 128, cc * 1024:(cc + 1) * 1024]),
                          "sSa" if n_load % 2 == 0 else "sSb", 16)
                    n_load += 1
                for cc in range(ncc):
                    p.w(sync, "sDV", b_dv + NVT * t + ncc + 3 + max(0, 2 * cc - 3))
                    p.inc(sync.dma_start(out=fb[n_load % 2][:, :],
                                         in_=d_w[t * 128:(t + 1) * 128, cc * 1024:(cc + 1) * 1024]),
                          "sSa" if n_load % 2 == 0 else "sSb", 16)
                    n_load += 1

        @block.vector
        def _(v):
            n_load = 0
            for t in range(oc):
                for cc in range(ncc):
                    n_load += 1
                    p.w(v, "sDV")
                    wload(v, n_load)
                    i0 = v.tensor_reduce(out=preps[:, cc:cc + 1], in_=fb[(n_load - 1) % 2][:, :],
                                         axis=AX.X, op=ALU.max, apply_absolute_value=True)
                    p.w(v, "sDV", p.inc(i0, "sDV"))
                if ncc > 1:
                    i0b = v.tensor_reduce(out=preps[:, 8:9], in_=preps[:, 0:ncc],
                                          axis=AX.X, op=ALU.max)
                else:
                    i0b = v.tensor_copy(out=preps[:, 8:9], in_=preps[:, 0:1])
                p.w(v, "sDV", p.inc(i0b, "sDV"))
                i1 = v.tensor_scalar(out=swout[:, t:t + 1], in0=preps[:, 8:9],
                                     scalar1=1e-8, scalar2=1.0 / 127.0, op0=ALU.max, op1=ALU.mult)
                p.w(v, "sDV", p.inc(i1, "sDV"))
                i2 = v.reciprocal(out=preps[:, 9:10], in_=swout[:, t:t + 1])
                p.w(v, "sDV", p.inc(i2, "sDV"))
                for cc in range(ncc):
                    n_load += 1
                    gidx = t * ncc + cc
                    p.w(v, "sDV")
                    wload(v, n_load)
                    if gidx >= 2:
                        p.w(v, "sGa" if gidx % 2 == 0 else "sGb", b_gp[gidx % 2] + 16 * (gidx // 2))
                    i3 = v.tensor_scalar(out=prepi[:, :], in0=fb[(n_load - 1) % 2][:, :],
                                         scalar1=preps[:, 9:10], scalar2=None, op0=ALU.mult)
                    p.w(v, "sDV", p.inc(i3, "sDV"))
                    i4 = v.tensor_scalar(out=ob[gidx % 2], in0=prepi[:, :],
                                         scalar1=1.0, scalar2=None, op0=ALU.mult)
                    p.inc(i4, "sDV")

        @block.gpsimd
        def _(g):
            for t in range(oc):
                for cc in range(ncc):
                    gidx = t * ncc + cc
                    p.w(g, "sDV", b_dv + NVT * t + ncc + 3 + 2 * cc + 2)
                    p.inc(g.dma_start(out=s_stage[t * 128:(t + 1) * 128, cc * 1024:(cc + 1) * 1024],
                                      in_=ob[gidx % 2]), "sGa" if gidx % 2 == 0 else "sGb", 16)

        @block.tensor
        def _(te):
            od, pm = ctx["onesd"], ctx["psm"]
            for t in range(oc):
                for d in range(2):
                    p.w(te, "sDV", b_dv + NVT * t + d * (NVT // 2) + 2)
                    for _r in range(4):
                        te.matmul(pm[:, :], od[:, :], od[:, :], start=True, stop=True)

    _transpose_bounce(ctx, s_stage, s_T, odim, idim)


def _collective_max(ctx, cols, grp_size=1):
    p, nc = ctx["p"], ctx["nc"]
    maxac, rowb, locs, gbc = ctx["maxac"], ctx["rowb"], ctx["locs"], ctx["gbc"]
    d_red, d_cin, d_cout = ctx["d_red"], ctx["d_cin"], ctx["d_cout"]
    k = len(cols)

    with nc.Block() as block:
        b_s = p.cnt["sS"]
        b_dv = p.cnt["sDV"]

        @block.sync
        def _(sync):
            for j, col in enumerate(cols):
                p.inc(sync.dma_start(out=d_red[:], in_=maxac[:, col:col + 1]), "sS", 16)
                p.w(sync, "sS", p.cnt["sS"])
                p.inc(sync.dma_start(out=rowb[:, :], in_=d_red[:]), "sS", 16)
                p.w(sync, "sS", p.cnt["sS"])
                p.w(sync, "sDV", b_dv + j + 1)
            p.w(sync, "sDV", b_dv + k)
            p.inc(sync.dma_start(out=d_cin[0:8], in_=locs[0:1, 0:8]), "sS", 16)
            p.w(sync, "sS")

        @block.vector
        def _(v):
            for j, col in enumerate(cols):
                p.w(v, "sS", b_s + 32 * (j + 1))
                p.inc(v.reduce_max(out=locs[:, j:j + 1], in_=rowb[:, :], axis=AX.X), "sDV")

    with nc.Block() as block:
        @block.gpsimd
        def _(g):
            p.fence(g)
            ins = g.collective_compute(
                "AllReduce", ALU.max,
                replica_groups=[list(range(8))],
                ins=[d_cin.ap().opt()], outs=[d_cout.ap().opt()])
            p.inc(ins, "sCC")
            p.w(g, "sCC", p.cnt["sCC"])
            for j, col in enumerate(cols):
                p.inc(g.dma_start(out=gbc[:, col:col + 1], in_=AP(d_cout, j, [[0, 128], [1, 1]])), "sG", 16)
            p.w(g, "sG", p.cnt["sG"])


def _absmax_tiles(ctx, v, tiles, col, signed_max=False):
    p = ctx["p"]
    maxac, scl = ctx["maxac"], ctx["scl"]
    for t_ap in tiles:
        i0 = v.tensor_reduce(out=scl[:, 15:16], in_=t_ap, axis=AX.X, op=ALU.max,
                             apply_absolute_value=not signed_max)
        p.w(v, "sDV", p.inc(i0, "sDV"))
        i1 = v.tensor_max(out=maxac[:, col:col + 1], in0=maxac[:, col:col + 1], in1=scl[:, 15:16])
        p.w(v, "sDV", p.inc(i1, "sDV"))


def _derive_scale(ctx, v, gcol, scol, rcol, qmax, eps):
    p = ctx["p"]
    gbc, scl = ctx["gbc"], ctx["scl"]
    i0 = v.tensor_scalar(out=scl[:, scol:scol + 1], in0=gbc[:, gcol:gcol + 1],
                         scalar1=1.0 / qmax, scalar2=eps, op0=ALU.mult, op1=ALU.max)
    p.w(v, "sDV", p.inc(i0, "sDV"))
    i1 = v.reciprocal(out=scl[:, rcol:rcol + 1], in_=scl[:, scol:scol + 1])
    p.w(v, "sDV", p.inc(i1, "sDV"))


def _ffn_l0(ctx, sp_up, sp_gate):
    p, nc = ctx["p"], ctx["nc"]
    x_res, xln, iti, scl, maxac = ctx["x_res"], ctx["xln"], ctx["iti"], ctx["scl"], ctx["maxac"]
    acc2, wst, sqt, ps = ctx["acc2"], ctx["wst"], ctx["sqt"], ctx["ps"]
    s_upT, s_gateT, s_downT = ctx["s_upT"], ctx["s_gateT"], ctx["s_downT"]
    sp_g2, sp_o = ctx["sp_g2"], ctx["sp_o"]

    _layernorm(ctx)

    with nc.Block() as block:
        @block.vector
        def _(v):
            i = v.memset(maxac[:, 0:1], 0.0)
            p.w(v, "sDV", p.inc(i, "sDV"))
            _absmax_tiles(ctx, v, [xln[c][:, :] for c in range(DC)], 0)
    _collective_max(ctx, [0])
    with nc.Block() as block:
        @block.vector
        def _(v):
            _derive_scale(ctx, v, 0, 0, 1, 32767.0, 1e-12)

    with nc.Block() as block:
        b_dvq = p.cnt["sDV"]

        @block.vector
        def _(v):
            for c in range(DC):
                for tt in range(NT):
                    j = (c * NT + tt) % 2
                    sl = xln[c][:, tt * TT:(tt + 1) * TT]
                    i0 = v.tensor_scalar(out=iti[j][:, :], in0=sl, scalar1=scl[:, 1:2],
                                         scalar2=None, op0=ALU.mult)
                    p.w(v, "sDV", p.inc(i0, "sDV"))
                    i1 = v.tensor_scalar(out=sl, in0=iti[j][:, :], scalar1=1.0,
                                         scalar2=None, op0=ALU.mult)
                    p.w(v, "sDV", p.inc(i1, "sDV"))

        @block.tensor
        def _(t):
            od, pm = ctx["onesd"], ctx["psm"]
            for k in range(DC * NT):
                p.w(t, "sDV", b_dvq + 2 * k + 1)
                t.matmul(pm[:, :], od[:, :], od[:, :], start=True, stop=True)

    _mm_spill(ctx, s_upT, sp_up, 1)
    _mm_spill(ctx, s_gateT, sp_gate, 2)
    _collective_max(ctx, [1, 2])
    with nc.Block() as block:
        @block.vector
        def _(v):
            _derive_scale(ctx, v, 1, 2, 3, 32767.0, 1e-12)
            _derive_scale(ctx, v, 2, 4, 5, 32767.0, 1e-12)

    _stream_g2(ctx, sp_gate, sp_g2)
    _collective_max(ctx, [3])
    with nc.Block() as block:
        @block.vector
        def _(v):
            _derive_scale(ctx, v, 3, 6, 7, 32767.0, 1e-12)
            i = v.tensor_mul(out=scl[:, 8:9], in0=scl[:, 6:7], in1=scl[:, 2:3])
            p.w(v, "sDV", p.inc(i, "sDV"))

    _stream_o(ctx, sp_g2, sp_up, sp_o)
    _collective_max(ctx, [4])
    with nc.Block() as block:
        @block.vector
        def _(v):
            _derive_scale(ctx, v, 4, 9, 10, 32767.0, 1e-12)
            i = v.tensor_scalar(out=scl[:, 11:12], in0=scl[:, 9:10], scalar1=0.5,
                                scalar2=None, op0=ALU.mult)
            p.w(v, "sDV", p.inc(i, "sDV"))

    _down_mm(ctx, sp_o)


def _mm_spill(ctx, wT, spill, maxcol):
    p, nc = ctx["p"], ctx["nc"]
    xln, scl, maxac, wst, ps = ctx["xln"], ctx["scl"], ctx["maxac"], ctx["wst"], ctx["ps"]
    stg = ctx["stg"]
    NB = 6
    wsem = ["sSa", "sSb", "sSc", "sSd"]
    wslot = [wst[0][:, 0:1024], wst[0][:, 1024:2048],
             wst[1][:, 0:1024], wst[1][:, 1024:2048]]

    with nc.Block() as block:
        b_sp = [p.cnt[s] for s in wsem]
        b_gp = [p.cnt["sGa"], p.cnt["sGb"]]
        b_pe = p.cnt["sPE"]
        b_ac = p.cnt["sAC"]
        b_dv = p.cnt["sDV"]

        mxc = ctx["mxc"]

        @block.vector
        def _(v):
            i = v.memset(mxc[:, :], 0.0)
            p.w(v, "sDV", p.inc(i, "sDV"))
            for fs in range(I0C):
                p.w(v, "sAC", b_ac + fs * NT + NT)
                i0 = v.tensor_reduce(out=mxc[:, fs % 32:fs % 32 + 1], in_=stg[fs % 2][:, :],
                                     axis=AX.X, op=ALU.max, apply_absolute_value=True)
                p.inc(i0, "sDV")
            p.w(v, "sDV")
            i1 = v.tensor_reduce(out=maxac[:, maxcol:maxcol + 1], in_=mxc[:, :],
                                 axis=AX.X, op=ALU.max)
            p.w(v, "sDV", p.inc(i1, "sDV"))

        @block.sync
        def _(sync):
            p.fence(sync)
            for fs in range(I0C):
                if fs >= 4:
                    p.w(sync, "sPE", b_pe + (fs - 3) * NT)
                p.inc(sync.dma_start(
                    out=wslot[fs % 4],
                    in_=AP(wT, fs * 128, [[I0, 128], [128 * I0, DC], [1, 128]])),
                    wsem[fs % 4], 16)

        @block.tensor
        def _(t):
            for fs in range(I0C):
                p.w(t, wsem[fs % 4], b_sp[fs % 4] + 16 * (fs // 4 + 1))
                for tt in range(NT):
                    k = fs * NT + tt
                    if k >= NB:
                        p.w(t, "sAC", b_ac + (k - NB) + 1)
                    for c in range(DC):
                        ins = t.matmul(ps[k % NB][:, :], wslot[fs % 4][:, c * 128:(c + 1) * 128],
                                       xln[c][:, tt * TT:(tt + 1) * TT],
                                       start=(c == 0), stop=(c == DC - 1))
                        if c == DC - 1:
                            p.inc(ins, "sPE")

        @block.scalar
        def _(s):
            for fs in range(I0C):
                for tt in range(NT):
                    k = fs * NT + tt
                    p.w(s, "sPE", b_pe + k + 1)
                    if fs >= 2 and tt == 0:
                        p.w(s, "sGa" if fs % 2 == 0 else "sGb", b_gp[fs % 2] + 16 * (fs // 2))
                        p.w(s, "sDV", b_dv + 1 + (fs - 1))
                    ins = s.activation(out=stg[fs % 2][:, tt * TT:(tt + 1) * TT],
                                       in_=ps[k % NB][:, :], func=AF.Copy, scale=scl[:, 0:1])
                    p.inc(ins, "sAC")

        @block.gpsimd
        def _(g):
            for fs in range(I0C):
                p.w(g, "sAC", b_ac + fs * NT + NT)
                p.inc(g.dma_start(out=spill[fs * 128:(fs + 1) * 128, :], in_=stg[fs % 2][:, :]),
                      "sGa" if fs % 2 == 0 else "sGb", 16)


def _stream_g2(ctx, sp_gate, sp_g2):
    p, nc = ctx["p"], ctx["nc"]
    acc2, iti, sqt, scl, maxac = ctx["acc2"], ctx["iti"], ctx["sqt"], ctx["scl"], ctx["maxac"]
    iti2 = ctx["iti2"]
    gin = [acc2[:, 0:T], acc2[:, T:2 * T]]
    gout = [acc2[:, 2 * T:3 * T], acc2[:, 3 * T:4 * T]]

    with nc.Block() as block:
        b_sp = [p.cnt["sSa"], p.cnt["sSb"]]
        b_gp = [p.cnt["sGa"], p.cnt["sGb"]]
        b_dv = p.cnt["sDV"]
        b_ac = p.cnt["sAC"]
        NV = 5

        @block.sync
        def _(sync):
            p.fence(sync)
            for fs in range(I0C):
                if fs >= 2:
                    p.w(sync, "sDV", b_dv + 1 + NV * ((fs - 1) * NT))
                p.inc(sync.dma_start(out=gin[fs % 2][:, :], in_=sp_gate[fs * 128:(fs + 1) * 128, :]),
                      "sSa" if fs % 2 == 0 else "sSb", 16)

        @block.vector
        def _(v):
            i = v.memset(ctx["mxc"][:, :], 0.0)
            p.w(v, "sDV", p.inc(i, "sDV"))
            for fs in range(I0C):
                p.w(v, "sSa" if fs % 2 == 0 else "sSb", b_sp[fs % 2] + 16 * (fs // 2 + 1))
                if fs >= 2:
                    p.w(v, "sGa" if fs % 2 == 0 else "sGb", b_gp[fs % 2] + 16 * (fs // 2))
                for tt in range(NT):
                    k = fs * NT + tt
                    j = k % 2
                    i0 = v.tensor_scalar(out=iti[j][:, :], in0=gin[fs % 2][:, tt * TT:(tt + 1) * TT],
                                         scalar1=scl[:, 5:6], scalar2=None, op0=ALU.mult)
                    c0 = p.inc(i0, "sDV")
                    p.w(v, "sAC", b_ac + k + 1)
                    i1 = v.tensor_scalar(out=ctx["iti2"][j][:, :], in0=sqt[j][:, :],
                                         scalar1=65536.0, scalar2=-32768.0, op0=ALU.mult, op1=ALU.add)
                    p.w(v, "sDV", p.inc(i1, "sDV"))
                    i2 = v.tensor_scalar(out=sqt[j][:, :], in0=ctx["iti2"][j][:, :],
                                         scalar1=1.0 / 65536.0, scalar2=0.5, op0=ALU.mult, op1=ALU.add)
                    p.w(v, "sDV", p.inc(i2, "sDV"))
                    i3 = v.scalar_tensor_tensor(out=gout[fs % 2][:, tt * TT:(tt + 1) * TT],
                                                in0=iti[j][:, :], scalar=scl[:, 4:5], in1=sqt[j][:, :],
                                                op0=ALU.mult, op1=ALU.mult)
                    p.w(v, "sDV", p.inc(i3, "sDV"))
                    if tt == NT - 1:
                        i4 = v.tensor_reduce(out=ctx["mxc"][:, fs % 32:fs % 32 + 1],
                                             in_=gout[fs % 2][:, :], axis=AX.X, op=ALU.max,
                                             apply_absolute_value=True)
                        p.w(v, "sDV", p.inc(i4, "sDV"))
                    else:
                        p.inc(v.engine_nop(), "sDV")
                        p.w(v, "sDV")
            i5 = v.tensor_reduce(out=maxac[:, 3:4], in_=ctx["mxc"][:, :], axis=AX.X, op=ALU.max)
            p.w(v, "sDV", p.inc(i5, "sDV"))

        @block.scalar
        def _(s):
            for fs in range(I0C):
                for tt in range(NT):
                    k = fs * NT + tt
                    j = k % 2
                    p.w(s, "sDV", b_dv + 1 + NV * k + 1)
                    ins = s.activation(out=sqt[j][:, :], in_=iti[j][:, :],
                                       func=AF.Sigmoid, scale=scl[:, 4:5])
                    p.inc(ins, "sAC")

        @block.tensor
        def _(t):
            od, pm = ctx["onesd"], ctx["psm"]
            for k in range(I0C * NT):
                p.w(t, "sDV", b_dv + 1 + NV * k)
                t.matmul(pm[:, :], od[:, :], od[:, :], start=True, stop=True)

        @block.gpsimd
        def _(g):
            for fs in range(I0C):
                p.w(g, "sDV", b_dv + 1 + NV * (fs * NT + NT))
                p.inc(g.dma_start(out=sp_g2[fs * 128:(fs + 1) * 128, :], in_=gout[fs % 2][:, :]),
                      "sGa" if fs % 2 == 0 else "sGb", 16)


def _stream_o(ctx, sp_g2, sp_up, sp_o):
    p, nc = ctx["p"], ctx["nc"]
    acc2, iti, iti2, scl, maxac = ctx["acc2"], ctx["iti"], ctx["iti2"], ctx["scl"], ctx["maxac"]
    ain = [acc2[:, 0:T], acc2[:, T:2 * T]]
    bin_ = [acc2[:, 2 * T:3 * T], acc2[:, 3 * T:4 * T]]
    oout = [acc2[:, 4 * T:5 * T], acc2[:, 5 * T:6 * T]]

    with nc.Block() as block:
        b_sp = [p.cnt["sSa"], p.cnt["sSb"]]
        b_gp = [p.cnt["sGa"], p.cnt["sGb"]]
        b_dv = p.cnt["sDV"]
        NV = 4

        @block.sync
        def _(sync):
            p.fence(sync)
            for fs in range(I0C):
                if fs >= 2:
                    p.w(sync, "sDV", b_dv + 1 + NV * ((fs - 1) * NT))
                sem = "sSa" if fs % 2 == 0 else "sSb"
                p.inc(sync.dma_start(out=ain[fs % 2][:, :], in_=sp_g2[fs * 128:(fs + 1) * 128, :]), sem, 16)
                p.inc(sync.dma_start(out=bin_[fs % 2][:, :], in_=sp_up[fs * 128:(fs + 1) * 128, :]), sem, 16)

        @block.vector
        def _(v):
            i = v.memset(ctx["mxc"][:, :], 0.0)
            p.w(v, "sDV", p.inc(i, "sDV"))
            for fs in range(I0C):
                p.w(v, "sSa" if fs % 2 == 0 else "sSb", b_sp[fs % 2] + 32 * (fs // 2 + 1))
                if fs >= 2:
                    p.w(v, "sGa" if fs % 2 == 0 else "sGb", b_gp[fs % 2] + 16 * (fs // 2))
                for tt in range(NT):
                    k = fs * NT + tt
                    j = k % 2
                    i0 = v.tensor_scalar(out=iti[j][:, :], in0=ain[fs % 2][:, tt * TT:(tt + 1) * TT],
                                         scalar1=scl[:, 7:8], scalar2=None, op0=ALU.mult)
                    p.w(v, "sDV", p.inc(i0, "sDV"))
                    i1 = v.tensor_scalar(out=iti2[j][:, :], in0=bin_[fs % 2][:, tt * TT:(tt + 1) * TT],
                                         scalar1=scl[:, 3:4], scalar2=None, op0=ALU.mult)
                    p.w(v, "sDV", p.inc(i1, "sDV"))
                    i2 = v.scalar_tensor_tensor(out=oout[fs % 2][:, tt * TT:(tt + 1) * TT],
                                                in0=iti[j][:, :], scalar=scl[:, 8:9], in1=iti2[j][:, :],
                                                op0=ALU.mult, op1=ALU.mult)
                    p.w(v, "sDV", p.inc(i2, "sDV"))
                    if tt == NT - 1:
                        i4 = v.tensor_reduce(out=ctx["mxc"][:, fs % 32:fs % 32 + 1],
                                             in_=oout[fs % 2][:, :], axis=AX.X, op=ALU.max,
                                             apply_absolute_value=True)
                        p.w(v, "sDV", p.inc(i4, "sDV"))
                    else:
                        p.inc(v.engine_nop(), "sDV")
                        p.w(v, "sDV")
            i5 = v.tensor_reduce(out=maxac[:, 4:5], in_=ctx["mxc"][:, :], axis=AX.X, op=ALU.max)
            p.w(v, "sDV", p.inc(i5, "sDV"))

        @block.gpsimd
        def _(g):
            for fs in range(I0C):
                p.w(g, "sDV", b_dv + 1 + NV * (fs * NT + NT))
                p.inc(g.dma_start(out=sp_o[fs * 128:(fs + 1) * 128, :], in_=oout[fs % 2][:, :]),
                      "sGa" if fs % 2 == 0 else "sGb", 16)

        @block.tensor
        def _(t):
            od, pm = ctx["onesd"], ctx["psm"]
            for k in range(I0C * NT):
                p.w(t, "sDV", b_dv + 1 + NV * k)
                t.matmul(pm[:, :], od[:, :], od[:, :], start=True, stop=True)


def _down_mm(ctx, sp_o):
    p, nc = ctx["p"], ctx["nc"]
    acc2, iti, scl, wst, ps, x_res = ctx["acc2"], ctx["iti"], ctx["scl"], ctx["wst"], ctx["ps"], ctx["x_res"]
    s_downT = ctx["s_downT"]
    oslot = [acc2[:, fs * TT:(fs + 1) * TT] for fs in range(I0C)]
    wsem = ["sSc", "sSd", "sSe", "sSf"]
    wslot = [wst[0][:, 0:I0], wst[1][:, 0:I0],
             ctx["abuf"][:, 0:I0], ctx["abuf"][:, 4096:4096 + I0]]
    osem = ["sSa", "sSb", "sSg", "sSh"]
    oload = [ctx["sqt"][0], ctx["sqt"][1],
             ctx["abuf"][:, 8192:8192 + TT], ctx["abuf"][:, 8192 + TT:8192 + 2 * TT]]
    with nc.Block() as block:
        b_so = [p.cnt[s] for s in osem]
        b_sw = [p.cnt[s] for s in wsem]
        b_pe = p.cnt["sPE"]
        b_dv = p.cnt["sDV"]
        DVT = 2 * I0C + DC
        PET = DC

        @block.sync
        def _(sync):
            p.fence(sync)
            for tt in range(NT):
                for fs in range(I0C):
                    gi = tt * I0C + fs
                    if gi >= 4:
                        p.w(sync, "sDV", b_dv + DVT * ((gi - 4) // I0C) + 2 * ((gi - 4) % I0C) + 1)
                    p.inc(sync.dma_start(out=oload[gi % 4][:, :],
                                         in_=sp_o[fs * 128:(fs + 1) * 128, tt * TT:(tt + 1) * TT]),
                          osem[gi % 4], 16)
                for m in range(DC):
                    k = tt * DC + m
                    if k >= 4:
                        p.w(sync, "sPE", b_pe + k - 3)
                    p.inc(sync.dma_start(
                        out=wslot[k % 4],
                        in_=AP(s_downT, m * 128, [[D, 128], [128 * D, I0C], [1, 128]])),
                        wsem[k % 4], 16)

        @block.vector
        def _(v):
            for tt in range(NT):
                for fs in range(I0C):
                    gi = tt * I0C + fs
                    if tt > 0 and fs == 0:
                        p.w(v, "sPE", b_pe + PET * tt)
                    p.w(v, osem[gi % 4], b_so[gi % 4] + 16 * (gi // 4 + 1))
                    i0 = v.tensor_scalar(out=iti[gi % 2][:, :], in0=oload[gi % 4][:, :],
                                         scalar1=scl[:, 10:11], scalar2=None, op0=ALU.mult)
                    p.w(v, "sDV", p.inc(i0, "sDV"))
                    i1 = v.tensor_scalar(out=oslot[fs][:, :], in0=iti[gi % 2][:, :],
                                         scalar1=1.0, scalar2=None, op0=ALU.mult)
                    p.w(v, "sDV", p.inc(i1, "sDV"))
                for m in range(DC):
                    p.w(v, "sPE", b_pe + PET * tt + m + 1)
                    ins = v.scalar_tensor_tensor(out=x_res[m][:, tt * TT:(tt + 1) * TT],
                                                 in0=ps[m % 4][:, :], scalar=scl[:, 11:12],
                                                 in1=x_res[m][:, tt * TT:(tt + 1) * TT],
                                                 op0=ALU.mult, op1=ALU.add)
                    p.inc(ins, "sDV")

        @block.tensor
        def _(t):
            od, pm = ctx["onesd"], ctx["psm"]
            for tt in range(NT):
                for kk in range(I0C):
                    p.w(t, "sDV", b_dv + DVT * tt + 2 * kk + 1)
                    t.matmul(pm[:, :], od[:, :], od[:, :], start=True, stop=True)
                for m in range(DC):
                    k = tt * DC + m
                    p.w(t, wsem[k % 4], b_sw[k % 4] + 16 * (k // 4 + 1))
                    p.w(t, "sDV", b_dv + DVT * tt + 2 * I0C)
                    if k >= 4:
                        p.w(t, "sDV", b_dv + DVT * ((k - 4) // DC) + 2 * I0C + (k - 4) % DC + 1)
                    for fs in range(I0C):
                        ins = t.matmul(ps[m % 4][:, :], wslot[k % 4][:, fs * 128:(fs + 1) * 128],
                                       oslot[fs][:, :], start=(fs == 0), stop=(fs == I0C - 1))
                        if fs == I0C - 1:
                            p.inc(ins, "sPE")


def _attn_l0(ctx):
    p, nc = ctx["p"], ctx["nc"]
    x_res, xln, abuf, acc2 = ctx["x_res"], ctx["xln"], ctx["abuf"], ctx["acc2"]
    wst, sqt, ps, tmpb = ctx["wst"], ctx["sqt"], ctx["ps"], ctx["tmpb"]
    tmpa = ctx["tmpa"]
    d_k0, d_wo0 = ctx["d_k0"], ctx["d_wo0"]
    onesr, constt = ctx["onesr"], ctx["constt"]
    VS = 1040
    QOF, KOF = 16640, 18688

    _layernorm(ctx)

    with nc.Block() as block:
        @block.gpsimd
        def _(g):
            for c in range(DC):
                p.inc(g.dma_start(out=wst[c // 4][:, (c % 4) * 1024:(c % 4 + 1) * 1024],
                                  in_=d_k0[2][c * 128:(c + 1) * 128, :]), "sG", 16)
            g.memset(AP(abuf, 64, [[20736, 128], [1040, 16], [65, 16], [1, 1]]), 1.0)

    with nc.Block() as block:
        b_pe = p.cnt["sPE"]
        b_ac = p.cnt["sAC"]

        @block.tensor
        def _(t):
            p.fence(t)
            for kt in range(NCH):
                for gh in range(2):
                    k = kt * 2 + gh
                    if k >= 4:
                        p.w(t, "sAC", b_ac + k - 3)
                    for c in range(DC):
                        ins = t.matmul(ps[k % 4][:, :],
                                       xln[c][:, kt * 128:(kt + 1) * 128],
                                       wst[c // 4][:, (c % 4) * 1024 + gh * 512:(c % 4) * 1024 + (gh + 1) * 512],
                                       start=(c == 0), stop=(c == DC - 1))
                        if c == DC - 1:
                            p.inc(ins, "sPE")

        @block.scalar
        def _(s):
            for kt in range(NCH):
                for gh in range(2):
                    k = kt * 2 + gh
                    p.w(s, "sPE", b_pe + k + 1)
                    ins = s.activation(
                        out=AP(abuf, kt * VS + gh * 8 * 65, [[20736, 128], [65, 8], [1, 64]]),
                        in_=ps[k % 4][:, :], func=AF.Copy)
                    p.inc(ins, "sAC")

    PEJ, ACJ, DVJ = 264, 136, 16
    with nc.Block() as block:
        b_pe = p.cnt["sPE"]
        b_ac = p.cnt["sAC"]
        b_dv = p.cnt["sDV"]
        b_sg = p.cnt["sG"]
        b_gp = [p.cnt["sGa"], p.cnt["sGb"]]

        @block.gpsimd
        def _(g):
            for j in range(DC):
                if j >= 2:
                    p.w(g, "sPE", b_pe + PEJ * (j - 2) + 8)
                wsl = wst[j % 2]
                gs = "sGa" if j % 2 == 0 else "sGb"
                p.inc(g.dma_start(out=wsl[:, 0:1024],
                                  in_=AP(d_k0[0], j * 128, [[1024, 128], [131072, 8], [1, 128]])), gs, 16)
                p.inc(g.dma_start(out=wsl[:, 1024:2048],
                                  in_=AP(d_k0[1], j * 128, [[1024, 128], [131072, 8], [1, 128]])), gs, 16)
                for hh in range(2):
                    h = 2 * j + hh
                    p.w(g, "sDV", b_dv + DVJ * j + hh * 8 + 8)
                    p.inc(g.dma_start(out=ctx["d_dend"][h, :],
                                      in_=AP(tmpb, 0, [[32 * 1024, 2], [1, 1024]])), "sG", 16)

        @block.tensor
        def _(t):
            for j in range(DC):
                bp, ba, bd = b_pe + PEJ * j, b_ac + ACJ * j, b_dv + DVJ * j
                p.w(t, "sGa" if j % 2 == 0 else "sGb", b_gp[j % 2] + 32 * (j // 2 + 1))
                for qk in range(2):
                    for tt in range(NT):
                        k = qk * NT + tt
                        if k >= 4:
                            p.w(t, "sAC", ba + k - 3)
                        elif j >= 1:
                            p.w(t, "sDV", bd - DVJ + 8 + 2 * k + 2)
                        for c in range(DC):
                            ins = t.matmul(ps[k % 4][:, :],
                                           wst[j % 2][:, qk * 1024 + c * 128:qk * 1024 + (c + 1) * 128],
                                           xln[c][:, tt * TT:(tt + 1) * TT],
                                           start=(c == 0), stop=(c == DC - 1))
                            if c == DC - 1:
                                p.inc(ins, "sPE")
                for hh in range(2):
                    h = 2 * j + hh
                    r0, r1 = hh * 64, hh * 64 + 64
                    for kt in range(NCH):
                        for tp in range(2):
                            k2 = kt * 2 + tp
                            for half in range(2):
                                tt = 2 * tp + half
                                p.w(t, "sAC", max(ba + 8,
                                                  ba + 8 + hh * 64 + 2 * k2 + half - 1))
                                ins = t.matmul(ps[4 + half][:, :],
                                               abuf[r0:r1, KOF + kt * 128:KOF + (kt + 1) * 128],
                                               abuf[r0:r1, QOF + tt * TT:QOF + (tt + 1) * TT],
                                               start=True, stop=True,
                                               tile_position=(r0, 0))
                                p.inc(ins, "sPE")
                            for half in range(2):
                                tt = 2 * tp + half
                                p.w(t, "sAC", ba + 8 + hh * 64 + 2 * k2 + half + 1)
                                if hh == 1 and kt == 0:
                                    p.w(t, "sDV", bd + 2 * tt + 2)
                                ins2 = t.matmul(ps[tt][0:65, :],
                                                abuf[:, kt * VS + h * 65:kt * VS + (h + 1) * 65],
                                                ctx["sq2"][:, half * TT:(half + 1) * TT],
                                                start=(kt == 0), stop=(kt == NCH - 1))
                                p.inc(ins2, "sPE")

        @block.scalar
        def _(s):
            for j in range(DC):
                bp, ba = b_pe + PEJ * j, b_ac + ACJ * j
                for qk in range(2):
                    for tt in range(NT):
                        k = qk * NT + tt
                        p.w(s, "sPE", bp + k + 1)
                        off = QOF if qk == 0 else KOF
                        ins = s.activation(out=abuf[:, off + tt * TT:off + (tt + 1) * TT],
                                           in_=ps[k % 4][:, :], func=AF.Copy)
                        p.inc(ins, "sAC")
                for hh in range(2):
                    for kt in range(NCH):
                        for tp in range(2):
                            k2 = kt * 2 + tp
                            for half in range(2):
                                p.w(s, "sPE", bp + 8 + hh * 128 + 4 * k2 + half + 1)
                                ins = s.activation(
                                    out=ctx["sq2"][:, half * TT:(half + 1) * TT],
                                    in_=ctx["psW"][:, half * TT:(half + 1) * TT],
                                    func=AF.Exp, scale=constt[:, 2:3])
                                p.inc(ins, "sAC")

        @block.vector
        def _(v):
            for j in range(DC):
                bp = b_pe + PEJ * j
                for hh in range(2):
                    r0, r1 = hh * 64, hh * 64 + 64
                    p.w(v, "sPE", bp + 8 + (hh + 1) * 128)
                    p.w(v, "sG", b_sg + 16 * (2 * j + hh))
                    for tt in range(NT):
                        i0 = v.tensor_scalar(out=acc2[r0:r1, j * T + tt * TT:j * T + (tt + 1) * TT],
                                             in0=ps[tt][0:64, :], scalar1=1.0, scalar2=None,
                                             op0=ALU.mult)
                        p.w(v, "sDV", p.inc(i0, "sDV"))
                        i1 = v.tensor_scalar(out=tmpb[(tt // 2) * 32:(tt // 2) * 32 + 1,
                                                      (tt % 2) * TT:(tt % 2 + 1) * TT],
                                             in0=ps[tt][64:65, :], scalar1=1.0, scalar2=None,
                                             op0=ALU.mult)
                        p.w(v, "sDV", p.inc(i1, "sDV"))

    with nc.Block() as block:
        @block.sync
        def _(sync):
            p.fence(sync)
            p.inc(sync.dma_start(out=tmpa[:, 0:256],
                                 in_=AP(ctx["d_dend"], 0, [[256, 128], [1, 256]])), "sS", 16)
            p.w(sync, "sS")

        @block.vector
        def _(v):
            p.w(v, "sS")
            i0 = v.reciprocal(out=tmpb[:, 0:256], in_=tmpa[:, 0:256])
            p.w(v, "sDV", p.inc(i0, "sDV"))

        @block.gpsimd
        def _(g):
            p.w(g, "sDV", p.cnt["sDV"])
            p.inc(g.dma_start(out=AP(ctx["d_rdend"], 0, [[256, 128], [1, 256]]),
                              in_=tmpb[:, 0:256]), "sG", 16)
            p.w(g, "sG")

    with nc.Block() as block:
        b_gp = [p.cnt["sGa"], p.cnt["sGb"]]
        b_pe = p.cnt["sPE"]
        b_ac = p.cnt["sAC"]
        b_dv = p.cnt["sDV"]

        @block.gpsimd
        def _(g):
            p.fence(g)
            for idx in range(NHEAD * NT):
                hh, tt = divmod(idx, NT)
                if idx >= 1:
                    p.w(g, "sPE", b_pe + idx)
                p.inc(g.dma_start(out=ctx["sqt"][0][0:1, :],
                                  in_=ctx["d_rdend"][hh, tt * TT:(tt + 1) * TT]),
                      "sGa" if idx % 2 == 0 else "sGb", 16)

        @block.tensor
        def _(t):
            for idx in range(NHEAD * NT):
                p.w(t, "sGa" if idx % 2 == 0 else "sGb", b_gp[idx % 2] + 16 * (idx // 2 + 1))
                if idx >= 2:
                    p.w(t, "sAC", b_ac + idx - 1)
                ins = t.matmul(ps[4 + (idx % 2)][:, :], onesr[0:1, :], ctx["sqt"][0][0:1, :],
                               start=True, stop=True)
                p.inc(ins, "sPE")

        @block.scalar
        def _(s):
            for idx in range(NHEAD * NT):
                hh, tt = divmod(idx, NT)
                rr0 = (hh % 2) * 64
                p.w(s, "sPE", b_pe + idx + 1)
                if idx >= 1:
                    p.w(s, "sDV", b_dv + idx)
                ins = s.activation(out=ctx["sq2"][rr0:rr0 + 64, TT:2 * TT],
                                   in_=ps[4 + (idx % 2)][rr0:rr0 + 64, :], func=AF.Copy)
                p.inc(ins, "sAC")

        @block.vector
        def _(v):
            for idx in range(NHEAD * NT):
                hh, tt = divmod(idx, NT)
                j2, h2 = divmod(hh, 2)
                rr0 = h2 * 64
                p.w(v, "sAC", b_ac + idx + 1)
                ins = v.tensor_tensor(out=acc2[rr0:rr0 + 64, j2 * T + tt * TT:j2 * T + (tt + 1) * TT],
                                      in0=acc2[rr0:rr0 + 64, j2 * T + tt * TT:j2 * T + (tt + 1) * TT],
                                      in1=ctx["sq2"][rr0:rr0 + 64, TT:2 * TT], op=ALU.mult)
                p.inc(ins, "sDV")
    
    with nc.Block() as block:
        b_gp = [p.cnt["sGa"], p.cnt["sGb"]]
        b_pe = p.cnt["sPE"]
        b_dv = p.cnt["sDV"]

        @block.gpsimd
        def _(g):
            for m in range(DC):
                if m >= 2:
                    p.w(g, "sPE", b_pe + (m - 1) * NT)
                p.inc(g.dma_start(out=wst[m % 2][:, 0:1024],
                                  in_=AP(d_wo0, m * 128, [[1024, 128], [131072, 8], [1, 128]])),
                      "sGa" if m % 2 == 0 else "sGb", 16)

        @block.tensor
        def _(t):
            for m in range(DC):
                p.w(t, "sGa" if m % 2 == 0 else "sGb", b_gp[m % 2] + 16 * (m // 2 + 1))
                for tt in range(NT):
                    k = m * NT + tt
                    if k >= 4:
                        p.w(t, "sDV", b_dv + k - 3)
                    for gc in range(DC):
                        ins = t.matmul(ps[k % 4][:, :], wst[m % 2][:, gc * 128:(gc + 1) * 128],
                                       acc2[:, gc * T + tt * TT:gc * T + (tt + 1) * TT],
                                       start=(gc == 0), stop=(gc == DC - 1))
                        if gc == DC - 1:
                            p.inc(ins, "sPE")

        @block.vector
        def _(v):
            for m in range(DC):
                for tt in range(NT):
                    k = m * NT + tt
                    p.w(v, "sPE", b_pe + k + 1)
                    ins = v.tensor_add(out=x_res[m][:, tt * TT:(tt + 1) * TT],
                                       in0=ps[k % 4][:, :],
                                       in1=x_res[m][:, tt * TT:(tt + 1) * TT])
                    p.inc(ins, "sDV")


def _ffn_l1(ctx, stop_after_h=False):
    p, nc = ctx["p"], ctx["nc"]
    x_res, xln, iti, scl, maxac = ctx["x_res"], ctx["xln"], ctx["iti"], ctx["scl"], ctx["maxac"]
    acc2, wst, sqt, ps, stg = ctx["acc2"], ctx["wst"], ctx["sqt"], ctx["ps"], ctx["stg"]
    tmpa, tmpb = ctx["tmpa"], ctx["tmpb"]
    sw1, sw2, b1t, b2t, b1q, b2q = ctx["sw1"], ctx["sw2"], ctx["b1t"], ctx["b2t"], ctx["b1q"], ctx["b2q"]
    s_w1T, s_w2T, sp_h = ctx["s_w1T"], ctx["s_w2T"], ctx["sp_h"]
    MAGIC = 12582912.0

    _layernorm(ctx)

    with nc.Block() as block:
        @block.vector
        def _(v):
            i = v.memset(maxac[:, 0:1], -3.0e38)
            p.w(v, "sDV", p.inc(i, "sDV"))
            _absmax_tiles(ctx, v, [xln[c][:, :] for c in range(DC)], 0, signed_max=True)
    _collective_max(ctx, [0])
    with nc.Block() as block:
        @block.vector
        def _(v):
            i0 = v.tensor_scalar(out=scl[:, 0:1], in0=ctx["gbc"][:, 0:1],
                                 scalar1=1e-8, scalar2=1.0 / 255.0, op0=ALU.max, op1=ALU.mult)
            p.w(v, "sDV", p.inc(i0, "sDV"))
            i1 = v.reciprocal(out=scl[:, 1:2], in_=scl[:, 0:1])
            p.w(v, "sDV", p.inc(i1, "sDV"))
            i2 = v.tensor_scalar(out=tmpa[:, 0:F1C], in0=sw1[:, :], scalar1=scl[:, 0:1],
                                 scalar2=None, op0=ALU.mult)
            p.w(v, "sDV", p.inc(i2, "sDV"))
            i3 = v.reciprocal(out=tmpa[:, F1C:2 * F1C], in_=tmpa[:, 0:F1C])
            p.w(v, "sDV", p.inc(i3, "sDV"))
            i4 = v.tensor_tensor(out=tmpb[:, 0:F1C], in0=b1t[:, :], in1=tmpa[:, F1C:2 * F1C], op=ALU.mult)
            p.w(v, "sDV", p.inc(i4, "sDV"))
            i5 = v.tensor_scalar(out=tmpb[:, 0:F1C], in0=tmpb[:, 0:F1C],
                                 scalar1=MAGIC, scalar2=-MAGIC, op0=ALU.add, op1=ALU.add)
            p.w(v, "sDV", p.inc(i5, "sDV"))
            i6 = v.tensor_tensor(out=b1q[:, :], in0=tmpb[:, 0:F1C], in1=tmpa[:, 0:F1C], op=ALU.mult)
            p.w(v, "sDV", p.inc(i6, "sDV"))

    with nc.Block() as block:
        b_dvq = p.cnt["sDV"]

        @block.vector
        def _(v):
            for c in range(DC):
                for tt in range(NT):
                    j = (c * NT + tt) % 2
                    sl = xln[c][:, tt * TT:(tt + 1) * TT]
                    i0 = v.tensor_scalar(out=iti[j][:, :], in0=sl, scalar1=scl[:, 1:2],
                                         scalar2=0.0, op0=ALU.mult, op1=ALU.max)
                    p.w(v, "sDV", p.inc(i0, "sDV"))
                    i1 = v.tensor_scalar(out=sl, in0=iti[j][:, :], scalar1=1.0,
                                         scalar2=None, op0=ALU.mult)
                    p.w(v, "sDV", p.inc(i1, "sDV"))

        @block.tensor
        def _(t):
            od, pm = ctx["onesd"], ctx["psm"]
            for k in range(DC * NT):
                p.w(t, "sDV", b_dvq + 2 * k + 1)
                t.matmul(pm[:, :], od[:, :], od[:, :], start=True, stop=True)

    wsem = ["sSa", "sSb", "sSc", "sSd"]
    wslot = [wst[0][:, 0:1024], wst[0][:, 1024:2048],
             wst[1][:, 0:1024], wst[1][:, 1024:2048]]
    with nc.Block() as block:
        b_sp = [p.cnt[s] for s in wsem]
        b_gp = [p.cnt["sGa"], p.cnt["sGb"]]
        b_pe = p.cnt["sPE"]
        b_ac = p.cnt["sAC"]
        b_dv = p.cnt["sDV"]
        NB = 6

        mxc = ctx["mxc"]

        @block.vector
        def _(v):
            i = v.memset(mxc[:, :], 0.0)
            p.w(v, "sDV", p.inc(i, "sDV"))
            for fs in range(F1C):
                p.w(v, "sAC", b_ac + fs * NT + NT)
                i0 = v.tensor_reduce(out=mxc[:, fs % 32:fs % 32 + 1], in_=stg[fs % 2][:, :],
                                     axis=AX.X, op=ALU.max, apply_absolute_value=True)
                p.inc(i0, "sDV")
            p.w(v, "sDV")
            i1 = v.tensor_reduce(out=maxac[:, 1:2], in_=mxc[:, :], axis=AX.X, op=ALU.max)
            p.w(v, "sDV", p.inc(i1, "sDV"))

        @block.sync
        def _(sync):
            p.fence(sync)
            for fs in range(F1C):
                if fs >= 4:
                    p.w(sync, "sPE", b_pe + (fs - 3) * NT)
                p.inc(sync.dma_start(
                    out=wslot[fs % 4],
                    in_=AP(s_w1T, fs * 128, [[F1, 128], [128 * F1, DC], [1, 128]])),
                    wsem[fs % 4], 16)

        @block.tensor
        def _(t):
            for fs in range(F1C):
                p.w(t, wsem[fs % 4], b_sp[fs % 4] + 16 * (fs // 4 + 1))
                for tt in range(NT):
                    k = fs * NT + tt
                    if k >= NB:
                        p.w(t, "sAC", b_ac + (k - NB) + 1)
                    for c in range(DC):
                        ins = t.matmul(ps[k % NB][:, :], wslot[fs % 4][:, c * 128:(c + 1) * 128],
                                       xln[c][:, tt * TT:(tt + 1) * TT],
                                       start=(c == 0), stop=(c == DC - 1))
                        if c == DC - 1:
                            p.inc(ins, "sPE")

        @block.scalar
        def _(s):
            for fs in range(F1C):
                for tt in range(NT):
                    k = fs * NT + tt
                    p.w(s, "sPE", b_pe + k + 1)
                    if fs >= 2 and tt == 0:
                        p.w(s, "sGa" if fs % 2 == 0 else "sGb", b_gp[fs % 2] + 16 * (fs // 2))
                        p.w(s, "sDV", b_dv + 1 + (fs - 1))
                    ins = s.activation(out=stg[fs % 2][:, tt * TT:(tt + 1) * TT],
                                       in_=ps[k % NB][:, :], func=AF.Relu,
                                       scale=tmpa[:, fs:fs + 1], bias=b1q[:, fs:fs + 1])
                    p.inc(ins, "sAC")

        @block.gpsimd
        def _(g):
            for fs in range(F1C):
                p.w(g, "sAC", b_ac + fs * NT + NT)
                p.inc(g.dma_start(out=sp_h[fs * 128:(fs + 1) * 128, :], in_=stg[fs % 2][:, :]),
                      "sGa" if fs % 2 == 0 else "sGb", 16)

    if stop_after_h:
        return
    _collective_max(ctx, [1])
    with nc.Block() as block:
        @block.vector
        def _(v):
            i0 = v.tensor_scalar(out=scl[:, 2:3], in0=ctx["gbc"][:, 1:2],
                                 scalar1=1e-8, scalar2=1.0 / 255.0, op0=ALU.max, op1=ALU.mult)
            p.w(v, "sDV", p.inc(i0, "sDV"))
            i1 = v.reciprocal(out=scl[:, 3:4], in_=scl[:, 2:3])
            p.w(v, "sDV", p.inc(i1, "sDV"))
            i2 = v.tensor_scalar(out=tmpa[:, 40:40 + DC], in0=sw2[:, :], scalar1=scl[:, 2:3],
                                 scalar2=None, op0=ALU.mult)
            p.w(v, "sDV", p.inc(i2, "sDV"))
            i3 = v.reciprocal(out=tmpa[:, 48:48 + DC], in_=tmpa[:, 40:40 + DC])
            p.w(v, "sDV", p.inc(i3, "sDV"))
            i4 = v.tensor_tensor(out=tmpb[:, 0:DC], in0=b2t[:, :], in1=tmpa[:, 48:48 + DC], op=ALU.mult)
            p.w(v, "sDV", p.inc(i4, "sDV"))
            i5 = v.tensor_scalar(out=tmpb[:, 0:DC], in0=tmpb[:, 0:DC],
                                 scalar1=MAGIC, scalar2=-MAGIC, op0=ALU.add, op1=ALU.add)
            p.w(v, "sDV", p.inc(i5, "sDV"))
            i6 = v.tensor_tensor(out=b2q[:, :], in0=tmpb[:, 0:DC], in1=tmpa[:, 40:40 + DC], op=ALU.mult)
            p.w(v, "sDV", p.inc(i6, "sDV"))
            i7 = v.tensor_scalar(out=b2q[:, :], in0=b2q[:, :], scalar1=0.5, scalar2=None, op0=ALU.mult)
            p.w(v, "sDV", p.inc(i7, "sDV"))
            i8 = v.tensor_scalar(out=tmpa[:, 40:40 + DC], in0=tmpa[:, 40:40 + DC],
                                 scalar1=0.5, scalar2=None, op0=ALU.mult)
            p.w(v, "sDV", p.inc(i8, "sDV"))

    oslot2 = [acc2[:, fs * TT:(fs + 1) * TT] for fs in range(F1C)]
    wsem = ["sSc", "sSd", "sSe", "sSf"]
    wslot = [wst[0][:, 0:F1], wst[1][:, 0:F1],
             ctx["abuf"][:, 0:F1], ctx["abuf"][:, 4096:4096 + F1]]
    osem = ["sSa", "sSb", "sSg", "sSh"]
    oload = [ctx["sqt"][0], ctx["sqt"][1],
             ctx["abuf"][:, 8192:8192 + TT], ctx["abuf"][:, 8192 + TT:8192 + 2 * TT]]
    with nc.Block() as block:
        b_so = [p.cnt[s] for s in osem]
        b_sw = [p.cnt[s] for s in wsem]
        b_pe = p.cnt["sPE"]
        b_ac = p.cnt["sAC"]
        b_dv = p.cnt["sDV"]
        DVT = 2 * F1C + DC
        PET = DC

        @block.sync
        def _(sync):
            p.fence(sync)
            for tt in range(NT):
                for fs in range(F1C):
                    gi = tt * F1C + fs
                    if gi >= 4:
                        p.w(sync, "sDV", b_dv + DVT * ((gi - 4) // F1C) + 2 * ((gi - 4) % F1C) + 1)
                    p.inc(sync.dma_start(out=oload[gi % 4][:, :],
                                         in_=sp_h[fs * 128:(fs + 1) * 128, tt * TT:(tt + 1) * TT]),
                          osem[gi % 4], 16)
                for m in range(DC):
                    k = tt * DC + m
                    if k >= 4:
                        p.w(sync, "sPE", b_pe + k - 3)
                    p.inc(sync.dma_start(
                        out=wslot[k % 4],
                        in_=AP(s_w2T, m * 128, [[D, 128], [128 * D, F1C], [1, 128]])),
                        wsem[k % 4], 16)

        @block.vector
        def _(v):
            for tt in range(NT):
                for fs in range(F1C):
                    gi = tt * F1C + fs
                    if tt > 0 and fs == 0:
                        p.w(v, "sPE", b_pe + PET * tt)
                    p.w(v, osem[gi % 4], b_so[gi % 4] + 16 * (gi // 4 + 1))
                    i0 = v.tensor_scalar(out=iti[gi % 2][:, :], in0=oload[gi % 4][:, :],
                                         scalar1=scl[:, 3:4], scalar2=None, op0=ALU.mult)
                    p.w(v, "sDV", p.inc(i0, "sDV"))
                    i1 = v.tensor_scalar(out=oslot2[fs][:, :], in0=iti[gi % 2][:, :],
                                         scalar1=1.0, scalar2=None, op0=ALU.mult)
                    p.w(v, "sDV", p.inc(i1, "sDV"))
                for m in range(DC):
                    k = tt * DC + m
                    p.w(v, "sAC", b_ac + k + 1)
                    ins = v.tensor_add(out=x_res[m][:, tt * TT:(tt + 1) * TT],
                                       in0=tmpb[:, (k % 2) * TT:(k % 2 + 1) * TT],
                                       in1=x_res[m][:, tt * TT:(tt + 1) * TT])
                    p.inc(ins, "sDV")

        @block.tensor
        def _(t):
            od, pm = ctx["onesd"], ctx["psm"]
            for tt in range(NT):
                for kk in range(F1C):
                    p.w(t, "sDV", b_dv + DVT * tt + 2 * kk + 1)
                    t.matmul(pm[:, :], od[:, :], od[:, :], start=True, stop=True)
                for m in range(DC):
                    k = tt * DC + m
                    p.w(t, wsem[k % 4], b_sw[k % 4] + 16 * (k // 4 + 1))
                    p.w(t, "sDV", b_dv + DVT * tt + 2 * F1C)
                    if k >= 4:
                        p.w(t, "sDV", b_dv + DVT * ((k - 4) // DC) + 2 * F1C + (k - 4) % DC + 1)
                    for fs in range(F1C):
                        ins = t.matmul(ps[m % 4][:, :], wslot[k % 4][:, fs * 128:(fs + 1) * 128],
                                       oslot2[fs][:, :], start=(fs == 0), stop=(fs == F1C - 1))
                        if fs == F1C - 1:
                            p.inc(ins, "sPE")

        @block.scalar
        def _(s):
            for tt in range(NT):
                for m in range(DC):
                    k = tt * DC + m
                    p.w(s, "sPE", b_pe + k + 1)
                    if k >= 2:
                        p.w(s, "sDV", b_dv + DVT * ((k - 2) // DC) + 2 * F1C + (k - 2) % DC + 1)
                    ins = s.activation(out=tmpb[:, (k % 2) * TT:(k % 2 + 1) * TT],
                                       in_=ps[m % 4][:, :],
                                       func=AF.Identity, scale=tmpa[:, 40 + m:41 + m],
                                       bias=b2q[:, m:m + 1])
                    p.inc(ins, "sAC")


def _attn_l1(ctx):
    p, nc = ctx["p"], ctx["nc"]
    x_res, xln, abuf, acc2 = ctx["x_res"], ctx["xln"], ctx["abuf"], ctx["acc2"]
    wst, sqt, ps, psm, pst = ctx["wst"], ctx["sqt"], ctx["ps"], ctx["psm"], ctx["pst"]
    tmpb, scl, zex, zexb = ctx["tmpb"], ctx["scl"], ctx["zex"], ctx["zexb"]
    battn, bkbc, bvbc, trimask, ident, ones1, constt = (
        ctx["battn"], ctx["bkbc"], ctx["bvbc"], ctx["trimask"], ctx["ident"], ctx["ones1"], ctx["constt"])
    d_wq, d_wk, d_wv, d_wo1 = ctx["d_wq"], ctx["d_wk"], ctx["d_wv"], ctx["d_wo1"]

    _layernorm(ctx)

    S_OF, Q_OF, K_OF, KT_OF, VT_OF, AT_OF, OF_OF = 0, 8192, 9216, 10240, 11264, 12304, 12432
    T1_OF, T2_OF, T3_OF, OT_OF = 13456, 13968, 14480, 14992
    WQ_OF, WK_OF, WO_OF = 0, 8192, 16384
    NCHW = 4
    sp_of = ctx["sp_o"]

    with nc.Block() as block:
        @block.gpsimd
        def _(g):
            for ds in range(DC):
                p.inc(g.dma_start(out=abuf[:, WQ_OF + ds * 1024:WQ_OF + (ds + 1) * 1024],
                                  in_=AP(d_wq, ds * 128, [[1024, 128], [131072, 8], [1, 128]])), "sG", 16)
                p.inc(g.dma_start(out=abuf[:, WK_OF + ds * 1024:WK_OF + (ds + 1) * 1024],
                                  in_=AP(d_wk, ds * 128, [[1024, 128], [131072, 8], [1, 128]])), "sG", 16)
            for c in range(DC):
                p.inc(g.dma_start(out=wst[c // 4][:, (c % 4) * 1024:(c % 4 + 1) * 1024],
                                  in_=d_wv[c * 128:(c + 1) * 128, :]), "sG", 16)
            p.w(g, "sG")

        @block.vector
        def _(v):
            for c in range(DC):
                v.memset(acc2[:, S_OF + c * 1024:S_OF + (c + 1) * 1024], 0.0)
            v.memset(zex[:, :], 0.0)
            i = v.memset(zexb[:, :], 0.0)
            p.w(v, "sDV", p.inc(i, "sDV"))

    qb = lambda ds: abuf[:, WQ_OF + ds * 1024:WQ_OF + (ds + 1) * 1024]
    kb = lambda ds: abuf[:, WK_OF + ds * 1024:WK_OF + (ds + 1) * 1024]
    Sb = lambda ds, h: acc2[:, S_OF + ds * 1024 + h * TT:S_OF + ds * 1024 + (h + 1) * TT]
    Qf = lambda ds: acc2[:, Q_OF + ds * 128:Q_OF + (ds + 1) * 128]
    Kf = lambda ds: acc2[:, K_OF + ds * 128:K_OF + (ds + 1) * 128]
    Ktm = lambda ds: acc2[:, KT_OF + ds * 128:KT_OF + (ds + 1) * 128]
    Ktm_h = lambda h: acc2[:, KT_OF + h * TT:KT_OF + (h + 1) * TT]
    Vtm_h = lambda h: acc2[:, VT_OF + h * TT:VT_OF + (h + 1) * TT]
    ATt = acc2[:, AT_OF:AT_OF + 128]
    Ofm = lambda ds: acc2[:, OF_OF + ds * 128:OF_OF + (ds + 1) * 128]
    Tt1 = acc2[:, T1_OF:T1_OF + 512]
    Tt2 = acc2[:, T2_OF:T2_OF + 512]
    Tt3 = acc2[:, T3_OF:T3_OF + 512]
    Otm = lambda h: acc2[:, OT_OF + h * TT:OT_OF + (h + 1) * TT]

    PE_N, DV_N, AC_N = 56, 117, 18
    B0 = dict(pe=p.cnt["sPE"], dv=p.cnt["sDV"], ac=p.cnt["sAC"])
    b0g = [p.cnt["sGa"], p.cnt["sGb"]]

    def base(n):
        return dict(pe=B0["pe"] + PE_N * n, dv=B0["dv"] + DV_N * n, ac=B0["ac"] + AC_N * n)

    with nc.Block() as block:
        @block.tensor
        def _(t):
            p.fence(t)
            for n in range(NCH):
                tok = slice(n * 128, (n + 1) * 128)
                B = base(n)
                for qk in range(2):
                    for ds in range(DC):
                        k = qk * DC + ds
                        if k >= 2:
                            p.w(t, "sDV", B["dv"] + 4 * (k - 2) + 2)
                        elif n > 0:
                            p.w(t, "sDV", base(n - 1)["dv"] + DV_N)
                        sl = ctx["psW"][:, 0:128] if k % 2 == 0 else ps[0][:, 0:128]
                        for c in range(DC):
                            ins = t.matmul(sl, (qb(ds) if qk == 0 else kb(ds))[:, c * 128:(c + 1) * 128],
                                           xln[c][:, tok], start=(c == 0), stop=(c == DC - 1))
                            if c == DC - 1:
                                p.inc(ins, "sPE")
                for kv in range(2):
                    for h in range(2):
                        k = kv * 2 + h
                        bank = ps[2 + (k % 2)]
                        if k >= 2:
                            p.w(t, "sDV", B["dv"] + 64 + (4 if k == 2 else 7))
                        elif n > 0:
                            p.w(t, "sDV", base(n - 1)["dv"] + DV_N)
                        for c in range(DC):
                            if kv == 0:
                                rhs = AP(ctx["abuf"], WK_OF + c * 128 + h * 4 * 1024,
                                         [[20736, 128], [1024, 4], [1, 128]])
                            else:
                                rhs = AP(wst[0] if (c < 4) else wst[1],
                                         (c % 4) * 1024 + h * 512, [[4096, 128], [1, 512]])
                            ins = t.matmul(bank[:, :], xln[c][:, tok], rhs,
                                           start=(c == 0), stop=(c == DC - 1))
                            if c == DC - 1:
                                p.inc(ins, "sPE")
                p.w(t, "sDV", B["dv"] + 64)
                for ds in range(DC):
                    ins = t.matmul(psm[:, :], Kf(ds), Qf(ds), start=(ds == 0), stop=(ds == DC - 1))
                    if ds == DC - 1:
                        p.inc(ins, "sPE")
                p.w(t, "sDV", B["dv"] + 73)
                for h in range(2):
                    for ds in range(DC):
                        ins = t.matmul(ps[h][:, :], Qf(ds), Sb(ds, h),
                                       start=(ds == 0), stop=False)
                    ins = t.matmul(ps[h][:, :], ATt, Vtm_h(h), start=False, stop=True)
                    p.inc(ins, "sPE")
                for ds in range(DC):
                    ins = t.matmul(psm[:, 0:1], Qf(ds), zexb[:, ds:ds + 1],
                                   start=(ds == 0), stop=False)
                ins = t.matmul(psm[:, 0:1], ATt, ones1[:, 0:1], start=False, stop=True)
                p.inc(ins, "sPE")
                for ds in range(DC):
                    p.w(t, "sDV", B["dv"] + 76 + (1 if ds >= 4 else 0))
                    if ds >= 1:
                        p.w(t, "sDV", B["dv"] + 77 + ds)
                    ins = t.transpose(pst[:, :], acc2[:, OT_OF + ds * 128:OT_OF + (ds + 1) * 128],
                                      ident[:, :])
                    p.inc(ins, "sPE")
                p.w(t, "sDV", B["dv"] + 72)
                for ds in range(DC):
                    for h in range(2):
                        k = ds * 2 + h
                        if k >= 2:
                            p.w(t, "sDV", B["dv"] + 85 + 4 * ((k - 2) // 2) + (k - 2) % 2 + 1)
                        bank = ps[2 + (k % 2)]
                        ins = t.matmul(bank[:, :], Ktm(ds), Vtm_h(h), start=True, stop=True)
                        p.inc(ins, "sPE")
                    if ds >= 1:
                        p.w(t, "sDV", B["dv"] + 85 + 4 * ds - 1)
                    else:
                        p.w(t, "sDV", B["dv"] + 74)
                    ins = t.matmul(psm[:, 1:2], Ktm(ds), ones1[:, 0:1], start=True, stop=True)
                    p.inc(ins, "sPE")

        @block.vector
        def _(v):
            for n in range(NCH):
                B = base(n)
                for qk in range(2):
                    biascol = 0 if qk == 0 else DC
                    for ds in range(DC):
                        k = qk * DC + ds
                        sl = ctx["psW"][:, 0:128] if k % 2 == 0 else ps[0][:, 0:128]
                        bias_ap = battn[:, biascol + ds:biascol + ds + 1]
                        p.w(v, "sPE", B["pe"] + k + 1)
                        p.w(v, "sDV")
                        p.w(v, "sAC", B["ac"] + k)
                        i0 = v.tensor_scalar(out=Tt1[:, 0:128], in0=sl, scalar1=bias_ap,
                                             scalar2=0.0, op0=ALU.add, op1=ALU.min)
                        p.w(v, "sDV", p.inc(i0, "sDV"))
                        p.w(v, "sAC", B["ac"] + k + 1)
                        i1 = v.tensor_scalar(out=Tt3[:, 0:128], in0=sl, scalar1=bias_ap,
                                             scalar2=0.0, op0=ALU.add, op1=ALU.max)
                        p.w(v, "sDV", p.inc(i1, "sDV"))
                        i2 = v.tensor_tensor(out=(Qf(ds) if qk == 0 else Kf(ds)),
                                             in0=Tt3[:, 0:128], in1=Tt2[:, 0:128], op=ALU.add)
                        p.w(v, "sDV", p.inc(i2, "sDV"))
                        p.inc(v.engine_nop(), "sDV")
                for kv in range(2):
                    for h in range(2):
                        k = kv * 2 + h
                        bank = ps[2 + (k % 2)]
                        p.w(v, "sPE", B["pe"] + 16 + k + 1)
                        p.w(v, "sDV")
                        if kv == 0 and h == 0:
                            p.w(v, "sAC", B["ac"] + 16)
                        bb = bkbc if kv == 0 else bvbc
                        if kv == 0:
                            i0 = v.tensor_tensor(out=Tt1[:, :], in0=bank[:, :],
                                                 in1=bb[:, h * TT:(h + 1) * TT], op=ALU.add)
                            p.w(v, "sDV", p.inc(i0, "sDV"))
                            i1 = v.tensor_scalar(out=Tt2[:, :], in0=Tt1[:, :],
                                                 scalar1=0.0, scalar2=None, op0=ALU.min)
                            p.w(v, "sDV", p.inc(i1, "sDV"))
                            p.w(v, "sAC", B["ac"] + 16 + (1 if h == 0 else 2))
                            i2 = v.scalar_tensor_tensor(out=Ktm_h(h), in0=Tt1[:, :], scalar=0.0,
                                                        in1=Tt3[:, :], op0=ALU.max, op1=ALU.add)
                            p.w(v, "sDV", p.inc(i2, "sDV"))
                        else:
                            i0 = v.tensor_tensor(out=Vtm_h(h), in0=bank[:, :],
                                                 in1=bb[:, h * TT:(h + 1) * TT], op=ALU.add)
                            p.w(v, "sDV", p.inc(i0, "sDV"))
                p.w(v, "sPE", B["pe"] + 21)
                i = v.tensor_tensor(out=ATt, in0=psm[:, :], in1=trimask[:, :], op=ALU.mult)
                p.w(v, "sDV", p.inc(i, "sDV"))
                p.w(v, "sPE", B["pe"] + 24)
                i0 = v.tensor_scalar(out=scl[:, 12:13], in0=psm[:, 0:1],
                                     scalar1=1e-6, scalar2=None, op0=ALU.add)
                p.w(v, "sDV", p.inc(i0, "sDV"))
                i1 = v.reciprocal(out=scl[:, 13:14], in_=scl[:, 12:13])
                p.w(v, "sDV", p.inc(i1, "sDV"))
                for h in range(2):
                    i2 = v.tensor_scalar(out=Otm(h), in0=ps[h][:, :],
                                         scalar1=scl[:, 13:14], scalar2=None, op0=ALU.mult)
                    p.w(v, "sDV", p.inc(i2, "sDV"))
                for ds in range(DC):
                    if ds == 0 and n >= 1:
                        p.w(v, "sGa", b0g[0] + 16 * ((n + 1) // 2))
                        p.w(v, "sGb", b0g[1] + 16 * (n // 2))
                    p.w(v, "sPE", B["pe"] + 24 + ds + 1)
                    i = v.tensor_scalar(out=Ofm(ds), in0=pst[:, :], scalar1=1.0,
                                        scalar2=None, op0=ALU.mult)
                    p.inc(i, "sDV")
                for ds in range(DC):
                    for h in range(2):
                        k = ds * 2 + h
                        bank = ps[2 + (k % 2)]
                        p.w(v, "sPE", B["pe"] + 32 + 3 * ds + h + 1)
                        p.w(v, "sDV")
                        i = v.tensor_tensor(out=Sb(ds, h), in0=bank[:, :], in1=Sb(ds, h), op=ALU.add)
                        p.w(v, "sDV", p.inc(i, "sDV"))
                    p.w(v, "sPE", B["pe"] + 32 + 3 * ds + 3)
                    i2 = v.tensor_tensor(out=zex[:, ds:ds + 1], in0=psm[:, 1:2],
                                         in1=zex[:, ds:ds + 1], op=ALU.add)
                    p.w(v, "sDV", p.inc(i2, "sDV"))
                    i3 = v.tensor_scalar(out=zexb[:, ds:ds + 1], in0=zex[:, ds:ds + 1],
                                         scalar1=1.0, scalar2=None, op0=ALU.mult)
                    p.w(v, "sDV", p.inc(i3, "sDV"))

        @block.scalar
        def _(s):
            for n in range(NCH):
                B = base(n)
                for qk in range(2):
                    for ds in range(DC):
                        k = qk * DC + ds
                        p.w(s, "sDV", B["dv"] + 4 * k + 1)
                        ins = s.activation(out=Tt2[:, 0:128], in_=Tt1[:, 0:128], func=AF.Exp)
                        p.inc(ins, "sAC")
                for h in range(2):
                    p.w(s, "sDV", B["dv"] + 64 + 3 * h + 2)
                    ins = s.activation(out=Tt3[:, :], in_=Tt2[:, :], func=AF.Exp)
                    p.inc(ins, "sAC")

        @block.gpsimd
        def _(g):
            for n in range(NCH):
                B = base(n)
                p.w(g, "sDV", B["dv"] + 85)
                p.inc(g.dma_start(out=AP(sp_of, n * 128, [[T, 128], [128 * T, 8], [1, 128]]),
                                  in_=acc2[:, OF_OF:OF_OF + 1024]),
                      "sGa" if n % 2 == 0 else "sGb", 16)
    assert p.cnt["sPE"] == B0["pe"] + PE_N * NCH, (p.cnt["sPE"], B0["pe"] + PE_N * NCH)
    assert p.cnt["sDV"] == B0["dv"] + DV_N * NCH, (p.cnt["sDV"], B0["dv"] + DV_N * NCH)
    assert p.cnt["sAC"] == B0["ac"] + AC_N * NCH

    with nc.Block() as block:
        b_snap = p.snap()
        b_sp = [p.cnt["sSa"], p.cnt["sSb"]]
        b_gp = [p.cnt["sGa"], p.cnt["sGb"]]
        b_pe = p.cnt["sPE"]
        b_ac = p.cnt["sAC"]
        b_dv = p.cnt["sDV"]

        @block.gpsimd
        def _(g):
            p.fence(g, b_snap)
            for m in range(DC):
                if m >= 2:
                    p.w(g, "sPE", b_pe + (m - 1) * NT)
                p.inc(g.dma_start(out=wst[m % 2][:, 0:1024],
                                  in_=AP(d_wo1, m * 128, [[1024, 128], [131072, 8], [1, 128]])),
                      "sGa" if m % 2 == 0 else "sGb", 16)

        @block.sync
        def _(sync):
            p.fence(sync, b_snap)
            for c in range(DC):
                p.inc(sync.dma_start(out=acc2[:, c * T:(c + 1) * T],
                                     in_=sp_of[c * 128:(c + 1) * 128, :]), "sS", 16)
            p.w(sync, "sS")

        @block.tensor
        def _(t):
            p.w(t, "sS")
            for m in range(DC):
                p.w(t, "sGa" if m % 2 == 0 else "sGb", b_gp[m % 2] + 16 * (m // 2 + 1))
                for tt in range(NT):
                    k = m * NT + tt
                    if k >= 4:
                        p.w(t, "sAC", b_ac + k - 3)
                    for ec in range(DC):
                        ins = t.matmul(ps[k % 4][:, :], wst[m % 2][:, ec * 128:(ec + 1) * 128],
                                       acc2[:, ec * T + tt * TT:ec * T + (tt + 1) * TT],
                                       start=(ec == 0), stop=(ec == DC - 1))
                        if ec == DC - 1:
                            p.inc(ins, "sPE")

        @block.scalar
        def _(s):
            for m in range(DC):
                for tt in range(NT):
                    k = m * NT + tt
                    p.w(s, "sPE", b_pe + k + 1)
                    if k >= 2:
                        p.w(s, "sDV", b_dv + k - 1)
                    ins = s.activation(out=sqt[k % 2][:, :], in_=ps[k % 4][:, :],
                                       func=AF.Identity, bias=battn[:, 3 * DC + m:3 * DC + m + 1],
                                       scale=1.0)
                    p.inc(ins, "sAC")

        @block.vector
        def _(v):
            for m in range(DC):
                for tt in range(NT):
                    k = m * NT + tt
                    p.w(v, "sAC", b_ac + k + 1)
                    ins = v.tensor_add(out=x_res[m][:, tt * TT:(tt + 1) * TT],
                                       in0=sqt[k % 2][:, :],
                                       in1=x_res[m][:, tt * TT:(tt + 1) * TT])
                    p.inc(ins, "sDV")



def _layout(inputs):
    x = np.asarray(inputs["L0_attn_x"])
    k0 = np.asarray(inputs["L0_attn_kernel"])
    wo0 = np.asarray(inputs["L0_attn_WO"])[0]
    base = {
        "k0_0": np.ascontiguousarray(k0[0].transpose(1, 0, 2).reshape(D, D)),
        "k0_1": np.ascontiguousarray(k0[1].transpose(1, 0, 2).reshape(D, D)),
        "k0_2": np.ascontiguousarray(k0[2].transpose(1, 0, 2).reshape(D, D)),
        "wo0": np.ascontiguousarray(wo0.reshape(DH, NHEAD, D).transpose(1, 0, 2).reshape(D, D)),
        "w_gate": np.asarray(inputs["L0_ffn_w_gate"]),
        "w_up": np.asarray(inputs["L0_ffn_w_up"]),
        "w_down": np.asarray(inputs["L0_ffn_w_down"]),
        "wqT": np.ascontiguousarray(np.asarray(inputs["L1_attn_Wq"]).T),
        "wkT": np.ascontiguousarray(np.asarray(inputs["L1_attn_Wk"]).T),
        "wvT": np.ascontiguousarray(np.asarray(inputs["L1_attn_Wv"]).T),
        "woT": np.ascontiguousarray(np.asarray(inputs["L1_attn_Wo"]).T),
        "battn4": np.ascontiguousarray(np.stack([
            np.asarray(inputs["L1_attn_bq"]), np.asarray(inputs["L1_attn_bk"]),
            np.asarray(inputs["L1_attn_bv"]), np.asarray(inputs["L1_attn_bo"])])),
        "w1": np.asarray(inputs["L1_ffn_w1"]),
        "b1": np.asarray(inputs["L1_ffn_b1"]),
        "w2": np.asarray(inputs["L1_ffn_w2"]),
        "b2": np.asarray(inputs["L1_ffn_b2"]),
    }
    maps = []
    for b in range(8):
        m = dict(base)
        m["x_t"] = np.ascontiguousarray(x[b].T)
        maps.append(m)
    return maps


def kernel(**inputs):
    maps = _layout(inputs)
    nc = build(upto=_UPTO, dbg=_DBG)
    res = run_bass_kernel_spmd(nc, maps, core_ids=list(range(8)))
    out = np.stack([np.ascontiguousarray(r["out_t"].T) for r in res.results])
    return out.astype(np.float32)


# version 12 (final)
# speedup vs baseline: 1.0610x; 1.0015x over previous
import contextlib

import numpy as np

import concourse.bass as bass
import concourse.mybir as mybir
from concourse.bass_utils import run_bass_kernel_spmd

F32 = mybir.dt.float32
BF16 = mybir.dt.bfloat16
I16 = mybir.dt.int16
ALU = mybir.AluOpType
AF = mybir.ActivationFunctionType
AP = bass.AP
AX = mybir.AxisListType

T = 2048
TT = 512
NT = T // TT
D = 1024
DC = D // 128
I0 = 3072
I0C = I0 // 128
F1 = 4096
F1C = F1 // 128
NHEAD = 16
DH = 64
CHUNK = 128
NCH = T // CHUNK

_UPTO = 6
_DBG = None


class P:

    def __init__(self, nc, es):
        self.nc = nc
        self.es = es
        self.cnt = {}
        self.sems = {}
        for s in ["sS", "sSa", "sSb", "sSc", "sSd", "sSe", "sSf", "sSg", "sSh",
                  "sG", "sGa", "sGb", "sGc", "sGd", "sC", "sPE", "sDV", "sAC", "sCC"]:
            self.sems[s] = es.enter_context(nc.semaphore(s))
            self.cnt[s] = 0
        self.dma_sems = ["sS", "sSa", "sSb", "sSc", "sSd", "sSe", "sSf", "sSg", "sSh",
                        "sG", "sGa", "sGb", "sGc", "sGd"]

    def inc(self, instr, name, n=1):
        instr.then_inc(self.sems[name], n)
        self.cnt[name] += n
        return self.cnt[name]

    def w(self, eng, name, val=None):
        eng.wait_ge(self.sems[name], self.cnt[name] if val is None else val)

    def fence(self, eng, snap=None):
        c = self.cnt if snap is None else snap
        for name in self.dma_sems:
            if c[name] > 0:
                eng.wait_ge(self.sems[name], c[name])

    def snap(self):
        return dict(self.cnt)


def build(upto=6, dbg=None, race=False):
    nc = bass.Bass("TRN2", target_bir_lowering=False, detect_race_conditions=race)
    es = contextlib.ExitStack()
    nc._es = es
    p = P(nc, es)

    d_x = nc.dram_tensor("x_t", [D, T], F32, kind="ExternalInput")
    d_k0 = [nc.dram_tensor(f"k0_{i}", [D, D], F32, kind="ExternalInput") for i in range(3)]
    d_wo0 = nc.dram_tensor("wo0", [D, D], F32, kind="ExternalInput")
    d_wgate = nc.dram_tensor("w_gate", [I0, D], F32, kind="ExternalInput")
    d_wup = nc.dram_tensor("w_up", [I0, D], F32, kind="ExternalInput")
    d_wdown = nc.dram_tensor("w_down", [D, I0], F32, kind="ExternalInput")
    d_wq = nc.dram_tensor("wqT", [D, D], F32, kind="ExternalInput")
    d_wk = nc.dram_tensor("wkT", [D, D], F32, kind="ExternalInput")
    d_wv = nc.dram_tensor("wvT", [D, D], F32, kind="ExternalInput")
    d_wo1 = nc.dram_tensor("woT", [D, D], F32, kind="ExternalInput")
    d_battn = nc.dram_tensor("battn4", [4, D], F32, kind="ExternalInput")
    d_w1 = nc.dram_tensor("w1", [F1, D], F32, kind="ExternalInput")
    d_b1 = nc.dram_tensor("b1", [F1], F32, kind="ExternalInput")
    d_w2 = nc.dram_tensor("w2", [D, F1], F32, kind="ExternalInput")
    d_b2 = nc.dram_tensor("b2", [D], F32, kind="ExternalInput")
    d_out = nc.dram_tensor("out_t", [D, T], F32, kind="ExternalOutput")

    s_stage_ud = nc.dram_tensor("s_stage_ud", [I0, D], BF16)
    s_stage_dn = nc.dram_tensor("s_stage_dn", [D, I0], BF16)
    s_stage_w1 = nc.dram_tensor("s_stage_w1", [F1, D], BF16)
    s_stage_w2 = nc.dram_tensor("s_stage_w2", [D, F1], BF16)
    sp_g2 = nc.dram_tensor("sp_g2", [I0, T], BF16)
    sp_o = nc.dram_tensor("sp_o", [I0, T], BF16)
    s_gateT = nc.dram_tensor("s_gateT", [D, I0], BF16)
    s_upT = nc.dram_tensor("s_upT", [D, I0], BF16)
    s_downT = nc.dram_tensor("s_downT", [I0, D], BF16)
    s_w1T = nc.dram_tensor("s_w1T", [D, F1], BF16)
    s_w2T = nc.dram_tensor("s_w2T", [F1, D], BF16)
    sp_up = nc.dram_tensor("sp_up", [I0, T], BF16)
    sp_gate = nc.dram_tensor("sp_gate", [I0, T], BF16)
    sp_h = nc.dram_tensor("sp_h", [F1, T], BF16)
    d_dend = nc.dram_tensor("d_dend", [NHEAD, T], F32)
    d_rdend = nc.dram_tensor("d_rdend", [NHEAD, T], F32)
    d_red = nc.dram_tensor("d_red", [128], F32)
    d_cin = nc.dram_tensor("d_cin", [8], F32)
    d_cout = nc.dram_tensor("d_cout", [8], F32)

    sb = lambda name, shape, dt: es.enter_context(nc.sbuf_tensor(name, shape, dt))
    x_res = [sb(f"xres{c}", [128, T], F32) for c in range(DC)]
    xln = [sb(f"xln{c}", [128, T], BF16) for c in range(DC)]
    abuf = sb("abuf", [128, 20736], BF16)
    acc2 = sb("acc2", [128, 16384], BF16)
    tmpa = sb("tmpa", [128, 1024], F32)
    tmpb = sb("tmpb", [128, 1024], F32)
    wst = [sb(f"wst{i}", [128, 4096], BF16) for i in range(2)]
    stg = [acc2[:, 0:T], acc2[:, T:2 * T]]
    iti = [sb(f"iti{i}", [128, TT], I16) for i in range(2)]
    iti2 = [sb(f"iti2{i}", [128, TT], I16) for i in range(2)]
    sq2 = sb("sq2", [128, 2 * TT], BF16)
    sqt = [sq2[:, 0:TT], sq2[:, TT:2 * TT]]
    prepi = sb("prepi", [128, 1024], I16)
    preps = sb("preps", [128, 64], F32)
    onesd = sb("onesd", [128, 128], BF16)
    ones1 = sb("ones1", [128, 1], BF16)
    onesr = sb("onesr", [1, 128], BF16)
    trimask = sb("trimask", [128, 128], BF16)
    ident = sb("ident", [128, 128], BF16)
    maxac = sb("maxac", [128, 8], F32)
    mxc = sb("mxc", [128, 32], F32)
    rowb = sb("rowb", [1, 128], F32)
    locs = sb("locs", [1, 8], F32)
    gbc = sb("gbc", [128, 8], F32)
    scl = sb("scl", [128, 16], F32)
    sw1 = sb("sw1", [128, F1C], F32)
    sw2 = sb("sw2", [128, DC], F32)
    b1t = sb("b1t", [128, F1C], F32)
    b2t = sb("b2t", [128, DC], F32)
    b1q = sb("b1q", [128, F1C], F32)
    b2q = sb("b2q", [128, DC], F32)
    battn = sb("battn", [128, 4 * DC], F32)
    bkbc = sb("bkbc", [128, D], BF16)
    bvbc = sb("bvbc", [128, D], BF16)
    zex = sb("zex", [128, DC], F32)
    zexb = sb("zexb", [128, DC], BF16)
    constt = sb("constt", [128, 8], F32)

    ps = [es.enter_context(nc.psum_tensor(f"ps{i}", [128, TT], F32)) for i in range(4)]
    psW = es.enter_context(nc.psum_tensor("psW", [128, 2 * TT], F32))
    ps.append(psW[:, 0:TT])
    ps.append(psW[:, TT:2 * TT])
    psm = es.enter_context(nc.psum_tensor("psm", [128, 128], F32))
    pst = es.enter_context(nc.psum_tensor("pst", [128, 128], BF16))

    ctx = dict(locals())

    with nc.Block() as block:
        @block.sync
        def _(sync):
            for c in range(DC):
                p.inc(sync.dma_start(out=x_res[c][:, :], in_=d_x[c * 128:(c + 1) * 128, :]), "sS", 16)
            for j in range(4):
                p.inc(sync.dma_start(
                    out=AP(battn, j * DC, [[4 * DC, 128], [1, DC]]),
                    in_=AP(d_battn, j * D, [[1, 128], [128, DC]]),
                    allow_slow_non_contiguous=True), "sS", 16)
            p.inc(sync.dma_start(out=AP(b1t, 0, [[F1C, 128], [1, F1C]]),
                                 in_=AP(d_b1, 0, [[1, 128], [128, F1C]]),
                                 allow_slow_non_contiguous=True), "sS", 16)
            p.inc(sync.dma_start(out=AP(b2t, 0, [[DC, 128], [1, DC]]),
                                 in_=AP(d_b2, 0, [[1, 128], [128, DC]]),
                                 allow_slow_non_contiguous=True), "sS", 16)
            p.w(sync, "sS")

        @block.gpsimd
        def _(g):
            p.inc(g.dma_start(out=bkbc[:, :], in_=AP(d_battn, D, [[0, 128], [1, D]])), "sG", 16)
            p.inc(g.dma_start(out=bvbc[:, :], in_=AP(d_battn, 2 * D, [[0, 128], [1, D]])), "sG", 16)
            g.memset(constt[:, 0:1], 1e-5)
            g.memset(constt[:, 1:2], 1e-6)
            g.memset(constt[:, 2:3], 1.0 / 65536.0)
            g.memset(onesd[:, :], 1.0 / 1024.0)
            g.memset(ones1[:, :], 1.0)
            g.memset(onesr[:, :], 1.0)
            i = g.memset(trimask[:, :], 1.0)
            p.w(g, "sCC", p.inc(i, "sCC"))
            g.affine_select(out=trimask[:, :], in_=trimask[:, :], pattern=[[1, 128]],
                            compare_op=ALU.is_ge, fill=0.0, base=0, channel_multiplier=-1)
            i = g.memset(ident[:, :], 1.0)
            p.w(g, "sCC", p.inc(i, "sCC"))
            g.affine_select(out=ident[:, :], in_=ident[:, :], pattern=[[-1, 128]],
                            compare_op=ALU.is_equal, fill=0.0, base=0, channel_multiplier=1)
            p.w(g, "sG")

        @block.vector
        def _(v):
            v.memset(maxac[:, :], 0.0)
            v.memset(zex[:, :], 0.0)
            v.memset(locs[:, :], 0.0)

    if upto > 0:
        _prep_l0(ctx, d_wup, s_stage_ud, s_upT, I0, D)
        _prep_l0(ctx, d_wgate, s_stage_ud, s_gateT, I0, D)
        _prep_l0(ctx, d_wdown, s_stage_dn, s_downT, D, I0)
    if upto > 3:
        _prep_l1(ctx, d_w1, s_stage_w1, s_w1T, F1, D, sw1)
        _prep_l1(ctx, d_w2, s_stage_w2, s_w2T, D, F1, sw2)

    if dbg == "xln":
        _layernorm(ctx)
    elif dbg == "l1h":
        _ffn_l0(ctx, sp_up, sp_gate)
        _attn_l0(ctx)
        _ffn_l0(ctx, sp_up, sp_gate)
        _ffn_l1(ctx, stop_after_h=True)
    else:
        if upto > 0:
            _ffn_l0(ctx, sp_up, sp_gate)
        if upto > 1:
            _attn_l0(ctx)
        if upto > 2:
            _ffn_l0(ctx, sp_up, sp_gate)
        if upto > 3:
            _ffn_l1(ctx)
        if upto > 4:
            _attn_l1(ctx)
        if upto > 5:
            _ffn_l1(ctx)

    src = xln if dbg == "xln" else x_res
    with nc.Block() as block:
        @block.gpsimd
        def _(g):
            if dbg == "l1h":
                p.inc(g.dma_start(out=d_out[:, :], in_=sp_h[0:1024, :]), "sG", 16)
            else:
                for c in range(DC):
                    p.inc(g.dma_start(out=d_out[c * 128:(c + 1) * 128, :], in_=src[c][:, :]), "sG", 16)
            g.wait_ge(p.sems["sG"], p.cnt["sG"])
    return nc


def _bcast_stats(ctx, src_of, func):
    p, nc = ctx["p"], ctx["nc"]
    sqt, onesd, ps = ctx["sqt"], ctx["onesd"], ctx["ps"]
    with nc.Block() as block:
        b_ac = p.cnt["sAC"]
        b_pe = p.cnt["sPE"]

        @block.scalar
        def _(s):
            for tt in range(NT):
                for c in range(DC):
                    i = tt * DC + c
                    if i >= 2:
                        p.w(s, "sPE", b_pe + i - 1)
                    ins = s.activation(out=sqt[i % 2][:, :], in_=src_of(c, tt), func=func)
                    p.inc(ins, "sAC")

        @block.tensor
        def _(t):
            for tt in range(NT):
                for c in range(DC):
                    i = tt * DC + c
                    p.w(t, "sAC", b_ac + i + 1)
                    ins = t.matmul(ps[tt][:, :], onesd[:, :], sqt[i % 2][:, :],
                                   start=(c == 0), stop=(c == DC - 1))
                    p.inc(ins, "sPE")


def _layernorm(ctx):
    p, nc = ctx["p"], ctx["nc"]
    x_res, xln, tmpa, tmpb = ctx["x_res"], ctx["xln"], ctx["tmpa"], ctx["tmpb"]
    ps = ctx["ps"]

    _bcast_stats(ctx, lambda c, tt: x_res[c][:, tt * TT:(tt + 1) * TT], AF.Copy)

    with nc.Block() as block:
        @block.vector
        def _(v):
            for tt in range(NT):
                for c in range(DC):
                    v.tensor_sub(out=xln[c][:, tt * TT:(tt + 1) * TT],
                                 in0=x_res[c][:, tt * TT:(tt + 1) * TT], in1=ps[tt][:, :])

    _bcast_stats(ctx, lambda c, tt: xln[c][:, tt * TT:(tt + 1) * TT], AF.Square)

    with nc.Block() as block:
        b_ac = p.cnt["sAC"]
        b_dv = p.cnt["sDV"]

        @block.scalar
        def _(s):
            for tt in range(NT):
                if tt >= 2:
                    p.w(s, "sDV", b_dv + 2 * tt - 3)
                ins = s.activation(out=tmpa[:, (tt % 2) * TT:(tt % 2 + 1) * TT], in_=ps[tt][:, :],
                                   func=AF.Sqrt, bias=ctx["constt"][:, 0:1])
                p.inc(ins, "sAC")

        @block.vector
        def _(v):
            for tt in range(NT):
                p.w(v, "sDV")
                p.w(v, "sAC", b_ac + tt + 1)
                ins = v.reciprocal(out=tmpb[:, (tt % 2) * TT:(tt % 2 + 1) * TT],
                                   in_=tmpa[:, (tt % 2) * TT:(tt % 2 + 1) * TT])
                cdv = p.inc(ins, "sDV")
                p.w(v, "sDV", cdv)
                for c in range(DC):
                    ins = v.tensor_mul(out=xln[c][:, tt * TT:(tt + 1) * TT],
                                       in0=xln[c][:, tt * TT:(tt + 1) * TT],
                                       in1=tmpb[:, (tt % 2) * TT:(tt % 2 + 1) * TT])
                    if c == DC - 1:
                        p.inc(ins, "sDV")


def _transpose_bounce(ctx, s_stage, s_T, odim, idim):
    p, nc = ctx["p"], ctx["nc"]
    wst, abuf = ctx["wst"], ctx["abuf"]
    slots = [wst[0][:, 0:odim], wst[1][:, 0:odim],
             abuf[:, 0:odim], abuf[:, 4096:4096 + odim]]
    ssem = ["sSa", "sSb", "sSc", "sSd"]
    gsem = ["sGa", "sGb", "sGc", "sGd"]
    snap = p.snap()
    with nc.Block() as block:
        b_sp = [p.cnt[s] for s in ssem]
        b_gp = [p.cnt[s] for s in gsem]
        ic = idim // 128

        def load(eng, t):
            if t >= 4:
                p.w(eng, gsem[t % 4], b_gp[t % 4] + 16 * (t // 4))
            p.inc(eng.dma_start(out=slots[t % 4],
                                in_=s_stage[0:odim, t * 128:(t + 1) * 128], transpose=True),
                  ssem[t % 4], 16)

        @block.sync
        def _(sync):
            p.fence(sync, snap)
            for t in range(ic):
                load(sync, t)

        @block.gpsimd
        def _(g):
            for t in range(ic):
                p.w(g, ssem[t % 4], b_sp[t % 4] + 16 * (t // 4 + 1))
                p.inc(g.dma_start(out=s_T[t * 128:(t + 1) * 128, 0:odim], in_=slots[t % 4]),
                      gsem[t % 4], 16)


def _prep_l0(ctx, d_w, s_stage, s_T, odim, idim):
    p, nc = ctx["p"], ctx["nc"]
    tmpa, tmpb, prepi, preps = ctx["tmpa"], ctx["tmpb"], ctx["prepi"], ctx["preps"]
    stg = ctx["stg"]
    oc, ncc = odim // 128, idim // 1024
    fb = [tmpa, tmpb]
    ob = [stg[0][:, 0:1024], stg[0][:, 1024:2048]]

    with nc.Block() as block:
        b_sp = [p.cnt["sSa"], p.cnt["sSb"]]
        b_gp = [p.cnt["sGa"], p.cnt["sGb"]]
        b_dv = p.cnt["sDV"]
        NV = 5

        @block.sync
        def _(sync):
            p.fence(sync)
            for i in range(oc * ncc):
                t, cc = divmod(i, ncc)
                if i >= 2:
                    p.w(sync, "sDV", b_dv + NV * (i - 2) + 4)
                p.inc(sync.dma_start(out=fb[i % 2][:, :],
                                     in_=d_w[t * 128:(t + 1) * 128, cc * 1024:(cc + 1) * 1024]),
                      "sSa" if i % 2 == 0 else "sSb", 16)

        @block.vector
        def _(v):
            for i in range(oc * ncc):
                p.w(v, "sDV")
                p.w(v, "sSa" if i % 2 == 0 else "sSb", b_sp[i % 2] + 16 * (i // 2 + 1))
                if i >= 2:
                    p.w(v, "sGa" if i % 2 == 0 else "sGb", b_gp[i % 2] + 16 * (i // 2))
                f = fb[i % 2]
                i0 = v.tensor_reduce(out=preps[:, 0:32], in_=AP(f, 0, [[1024, 128], [32, 32], [1, 32]]),
                                     axis=AX.X, op=ALU.max, apply_absolute_value=True)
                p.w(v, "sDV", p.inc(i0, "sDV"))
                i1 = v.tensor_scalar(out=preps[:, 0:32], in0=preps[:, 0:32],
                                     scalar1=1.0 / 7.0, scalar2=1e-12, op0=ALU.mult, op1=ALU.max)
                p.w(v, "sDV", p.inc(i1, "sDV"))
                i2 = v.reciprocal(out=preps[:, 32:64], in_=preps[:, 0:32])
                p.w(v, "sDV", p.inc(i2, "sDV"))
                i3 = v.tensor_tensor(out=prepi[:, :], in0=f[:, :],
                                     in1=AP(preps, 32, [[64, 128], [1, 32], [0, 32]]), op=ALU.mult)
                p.w(v, "sDV", p.inc(i3, "sDV"))
                i4 = v.scalar_tensor_tensor(out=ob[i % 2], in0=prepi[:, :], scalar=1.0,
                                            in1=AP(preps, 0, [[64, 128], [1, 32], [0, 32]]),
                                            op0=ALU.mult, op1=ALU.mult)
                p.inc(i4, "sDV")

        @block.gpsimd
        def _(g):
            for i in range(oc * ncc):
                t, cc = divmod(i, ncc)
                p.w(g, "sDV", b_dv + NV * (i + 1))
                p.inc(g.dma_start(out=s_stage[t * 128:(t + 1) * 128, cc * 1024:(cc + 1) * 1024],
                                  in_=ob[i % 2]), "sGa" if i % 2 == 0 else "sGb", 16)

        @block.tensor
        def _(t):
            od, pm = ctx["onesd"], ctx["psm"]
            for i in range(oc * ncc):
                for d in range(2):
                    p.w(t, "sDV", b_dv + NV * i + 2 * d + 2)
                    for _r in range(4):
                        t.matmul(pm[:, :], od[:, :], od[:, :], start=True, stop=True)

    _transpose_bounce(ctx, s_stage, s_T, odim, idim)


def _prep_l1(ctx, d_w, s_stage, s_T, odim, idim, swout):
    p, nc = ctx["p"], ctx["nc"]
    tmpa, tmpb, prepi, preps = ctx["tmpa"], ctx["tmpb"], ctx["prepi"], ctx["preps"]
    stg = ctx["stg"]
    oc, ncc = odim // 128, idim // 1024
    fb = [tmpa, tmpb]
    ob = [stg[0][:, 0:1024], stg[0][:, 1024:2048]]

    with nc.Block() as block:
        b_sp = [p.cnt["sSa"], p.cnt["sSb"]]
        b_gp = [p.cnt["sGa"], p.cnt["sGb"]]
        b_dv = p.cnt["sDV"]
        NVT = 3 * ncc + 3

        def wload(eng, n_load):
            i = n_load - 1
            p.w(eng, "sSa" if i % 2 == 0 else "sSb", b_sp[i % 2] + 16 * (i // 2 + 1))

        @block.sync
        def _(sync):
            p.fence(sync)
            n_load = 0
            for t in range(oc):
                for cc in range(ncc):
                    if t > 0 or cc >= 2:
                        if cc < 2:
                            p.w(sync, "sDV", b_dv + NVT * t)
                        else:
                            p.w(sync, "sDV", b_dv + NVT * t + cc - 1)
                    p.inc(sync.dma_start(out=fb[n_load % 2][:, :],
                                         in_=d_w[t * 128:(t + 1) * 128, cc * 1024:(cc + 1) * 1024]),
                          "sSa" if n_load % 2 == 0 else "sSb", 16)
                    n_load += 1
                for cc in range(ncc):
                    p.w(sync, "sDV", b_dv + NVT * t + ncc + 3 + max(0, 2 * cc - 3))
                    p.inc(sync.dma_start(out=fb[n_load % 2][:, :],
                                         in_=d_w[t * 128:(t + 1) * 128, cc * 1024:(cc + 1) * 1024]),
                          "sSa" if n_load % 2 == 0 else "sSb", 16)
                    n_load += 1

        @block.vector
        def _(v):
            n_load = 0
            for t in range(oc):
                for cc in range(ncc):
                    n_load += 1
                    p.w(v, "sDV")
                    wload(v, n_load)
                    i0 = v.tensor_reduce(out=preps[:, cc:cc + 1], in_=fb[(n_load - 1) % 2][:, :],
                                         axis=AX.X, op=ALU.max, apply_absolute_value=True)
                    p.w(v, "sDV", p.inc(i0, "sDV"))
                if ncc > 1:
                    i0b = v.tensor_reduce(out=preps[:, 8:9], in_=preps[:, 0:ncc],
                                          axis=AX.X, op=ALU.max)
                else:
                    i0b = v.tensor_copy(out=preps[:, 8:9], in_=preps[:, 0:1])
                p.w(v, "sDV", p.inc(i0b, "sDV"))
                i1 = v.tensor_scalar(out=swout[:, t:t + 1], in0=preps[:, 8:9],
                                     scalar1=1e-8, scalar2=1.0 / 127.0, op0=ALU.max, op1=ALU.mult)
                p.w(v, "sDV", p.inc(i1, "sDV"))
                i2 = v.reciprocal(out=preps[:, 9:10], in_=swout[:, t:t + 1])
                p.w(v, "sDV", p.inc(i2, "sDV"))
                for cc in range(ncc):
                    n_load += 1
                    gidx = t * ncc + cc
                    p.w(v, "sDV")
                    wload(v, n_load)
                    if gidx >= 2:
                        p.w(v, "sGa" if gidx % 2 == 0 else "sGb", b_gp[gidx % 2] + 16 * (gidx // 2))
                    i3 = v.tensor_scalar(out=prepi[:, :], in0=fb[(n_load - 1) % 2][:, :],
                                         scalar1=preps[:, 9:10], scalar2=None, op0=ALU.mult)
                    p.w(v, "sDV", p.inc(i3, "sDV"))
                    i4 = v.tensor_scalar(out=ob[gidx % 2], in0=prepi[:, :],
                                         scalar1=1.0, scalar2=None, op0=ALU.mult)
                    p.inc(i4, "sDV")

        @block.gpsimd
        def _(g):
            for t in range(oc):
                for cc in range(ncc):
                    gidx = t * ncc + cc
                    p.w(g, "sDV", b_dv + NVT * t + ncc + 3 + 2 * cc + 2)
                    p.inc(g.dma_start(out=s_stage[t * 128:(t + 1) * 128, cc * 1024:(cc + 1) * 1024],
                                      in_=ob[gidx % 2]), "sGa" if gidx % 2 == 0 else "sGb", 16)

        @block.tensor
        def _(te):
            od, pm = ctx["onesd"], ctx["psm"]
            for t in range(oc):
                for d in range(2):
                    p.w(te, "sDV", b_dv + NVT * t + d * (NVT // 2) + 2)
                    for _r in range(4):
                        te.matmul(pm[:, :], od[:, :], od[:, :], start=True, stop=True)

    _transpose_bounce(ctx, s_stage, s_T, odim, idim)


def _collective_max(ctx, cols, grp_size=1):
    p, nc = ctx["p"], ctx["nc"]
    maxac, rowb, locs, gbc = ctx["maxac"], ctx["rowb"], ctx["locs"], ctx["gbc"]
    d_red, d_cin, d_cout = ctx["d_red"], ctx["d_cin"], ctx["d_cout"]
    k = len(cols)

    with nc.Block() as block:
        b_s = p.cnt["sS"]
        b_dv = p.cnt["sDV"]

        @block.sync
        def _(sync):
            for j, col in enumerate(cols):
                p.inc(sync.dma_start(out=d_red[:], in_=maxac[:, col:col + 1]), "sS", 16)
                p.w(sync, "sS", p.cnt["sS"])
                p.inc(sync.dma_start(out=rowb[:, :], in_=d_red[:]), "sS", 16)
                p.w(sync, "sS", p.cnt["sS"])
                p.w(sync, "sDV", b_dv + j + 1)
            p.w(sync, "sDV", b_dv + k)
            p.inc(sync.dma_start(out=d_cin[0:8], in_=locs[0:1, 0:8]), "sS", 16)
            p.w(sync, "sS")

        @block.vector
        def _(v):
            for j, col in enumerate(cols):
                p.w(v, "sS", b_s + 32 * (j + 1))
                p.inc(v.reduce_max(out=locs[:, j:j + 1], in_=rowb[:, :], axis=AX.X), "sDV")

    with nc.Block() as block:
        @block.gpsimd
        def _(g):
            ins = g.collective_compute(
                "AllReduce", ALU.max,
                replica_groups=[list(range(8))],
                ins=[d_cin.ap().opt()], outs=[d_cout.ap().opt()])
            p.inc(ins, "sCC")
            p.w(g, "sCC", p.cnt["sCC"])
            for j, col in enumerate(cols):
                p.inc(g.dma_start(out=gbc[:, col:col + 1], in_=AP(d_cout, j, [[0, 128], [1, 1]])), "sG", 16)
            p.w(g, "sG", p.cnt["sG"])


def _absmax_tiles(ctx, v, tiles, col, signed_max=False):
    p = ctx["p"]
    maxac, scl = ctx["maxac"], ctx["scl"]
    for t_ap in tiles:
        i0 = v.tensor_reduce(out=scl[:, 15:16], in_=t_ap, axis=AX.X, op=ALU.max,
                             apply_absolute_value=not signed_max)
        p.w(v, "sDV", p.inc(i0, "sDV"))
        i1 = v.tensor_max(out=maxac[:, col:col + 1], in0=maxac[:, col:col + 1], in1=scl[:, 15:16])
        p.w(v, "sDV", p.inc(i1, "sDV"))


def _derive_scale(ctx, v, gcol, scol, rcol, qmax, eps):
    p = ctx["p"]
    gbc, scl = ctx["gbc"], ctx["scl"]
    i0 = v.tensor_scalar(out=scl[:, scol:scol + 1], in0=gbc[:, gcol:gcol + 1],
                         scalar1=1.0 / qmax, scalar2=eps, op0=ALU.mult, op1=ALU.max)
    p.w(v, "sDV", p.inc(i0, "sDV"))
    i1 = v.reciprocal(out=scl[:, rcol:rcol + 1], in_=scl[:, scol:scol + 1])
    p.w(v, "sDV", p.inc(i1, "sDV"))


def _ffn_l0(ctx, sp_up, sp_gate):
    p, nc = ctx["p"], ctx["nc"]
    x_res, xln, iti, scl, maxac = ctx["x_res"], ctx["xln"], ctx["iti"], ctx["scl"], ctx["maxac"]
    acc2, wst, sqt, ps = ctx["acc2"], ctx["wst"], ctx["sqt"], ctx["ps"]
    s_upT, s_gateT, s_downT = ctx["s_upT"], ctx["s_gateT"], ctx["s_downT"]
    sp_g2, sp_o = ctx["sp_g2"], ctx["sp_o"]

    _layernorm(ctx)

    with nc.Block() as block:
        @block.vector
        def _(v):
            i = v.memset(maxac[:, 0:1], 0.0)
            p.w(v, "sDV", p.inc(i, "sDV"))
            _absmax_tiles(ctx, v, [xln[c][:, :] for c in range(DC)], 0)
    _collective_max(ctx, [0])
    with nc.Block() as block:
        @block.vector
        def _(v):
            _derive_scale(ctx, v, 0, 0, 1, 32767.0, 1e-12)

    with nc.Block() as block:
        b_dvq = p.cnt["sDV"]

        @block.vector
        def _(v):
            for c in range(DC):
                for tt in range(NT):
                    j = (c * NT + tt) % 2
                    sl = xln[c][:, tt * TT:(tt + 1) * TT]
                    i0 = v.tensor_scalar(out=iti[j][:, :], in0=sl, scalar1=scl[:, 1:2],
                                         scalar2=None, op0=ALU.mult)
                    p.w(v, "sDV", p.inc(i0, "sDV"))
                    i1 = v.tensor_scalar(out=sl, in0=iti[j][:, :], scalar1=1.0,
                                         scalar2=None, op0=ALU.mult)
                    p.w(v, "sDV", p.inc(i1, "sDV"))

        @block.tensor
        def _(t):
            od, pm = ctx["onesd"], ctx["psm"]
            for k in range(DC * NT):
                p.w(t, "sDV", b_dvq + 2 * k + 1)
                t.matmul(pm[:, :], od[:, :], od[:, :], start=True, stop=True)

    _mm_spill(ctx, s_upT, sp_up, 1)
    _mm_spill(ctx, s_gateT, sp_gate, 2)
    _collective_max(ctx, [1, 2])
    with nc.Block() as block:
        @block.vector
        def _(v):
            _derive_scale(ctx, v, 1, 2, 3, 32767.0, 1e-12)
            _derive_scale(ctx, v, 2, 4, 5, 32767.0, 1e-12)

    _stream_g2(ctx, sp_gate, sp_g2)
    _collective_max(ctx, [3])
    with nc.Block() as block:
        @block.vector
        def _(v):
            _derive_scale(ctx, v, 3, 6, 7, 32767.0, 1e-12)
            i = v.tensor_mul(out=scl[:, 8:9], in0=scl[:, 6:7], in1=scl[:, 2:3])
            p.w(v, "sDV", p.inc(i, "sDV"))

    _stream_o(ctx, sp_g2, sp_up, sp_o)
    _collective_max(ctx, [4])
    with nc.Block() as block:
        @block.vector
        def _(v):
            _derive_scale(ctx, v, 4, 9, 10, 32767.0, 1e-12)
            i = v.tensor_scalar(out=scl[:, 11:12], in0=scl[:, 9:10], scalar1=0.5,
                                scalar2=None, op0=ALU.mult)
            p.w(v, "sDV", p.inc(i, "sDV"))

    _down_mm(ctx, sp_o)


def _mm_spill(ctx, wT, spill, maxcol):
    p, nc = ctx["p"], ctx["nc"]
    xln, scl, maxac, wst, ps = ctx["xln"], ctx["scl"], ctx["maxac"], ctx["wst"], ctx["ps"]
    stg = ctx["stg"]
    NB = 6
    wsem = ["sSa", "sSb", "sSc", "sSd"]
    wslot = [wst[0][:, 0:1024], wst[0][:, 1024:2048],
             wst[1][:, 0:1024], wst[1][:, 1024:2048]]

    with nc.Block() as block:
        b_sp = [p.cnt[s] for s in wsem]
        b_gp = [p.cnt["sGa"], p.cnt["sGb"]]
        b_pe = p.cnt["sPE"]
        b_ac = p.cnt["sAC"]
        b_dv = p.cnt["sDV"]

        mxc = ctx["mxc"]

        @block.vector
        def _(v):
            i = v.memset(mxc[:, :], 0.0)
            p.w(v, "sDV", p.inc(i, "sDV"))
            for fs in range(I0C):
                p.w(v, "sAC", b_ac + fs * NT + NT)
                i0 = v.tensor_reduce(out=mxc[:, fs % 32:fs % 32 + 1], in_=stg[fs % 2][:, :],
                                     axis=AX.X, op=ALU.max, apply_absolute_value=True)
                p.inc(i0, "sDV")
            p.w(v, "sDV")
            i1 = v.tensor_reduce(out=maxac[:, maxcol:maxcol + 1], in_=mxc[:, :],
                                 axis=AX.X, op=ALU.max)
            p.w(v, "sDV", p.inc(i1, "sDV"))

        @block.sync
        def _(sync):
            p.fence(sync)
            for fs in range(I0C):
                if fs >= 4:
                    p.w(sync, "sPE", b_pe + (fs - 3) * NT)
                p.inc(sync.dma_start(
                    out=wslot[fs % 4],
                    in_=AP(wT, fs * 128, [[I0, 128], [128 * I0, DC], [1, 128]])),
                    wsem[fs % 4], 16)

        @block.tensor
        def _(t):
            for fs in range(I0C):
                p.w(t, wsem[fs % 4], b_sp[fs % 4] + 16 * (fs // 4 + 1))
                for tt in range(NT):
                    k = fs * NT + tt
                    if k >= NB:
                        p.w(t, "sAC", b_ac + (k - NB) + 1)
                    for c in range(DC):
                        ins = t.matmul(ps[k % NB][:, :], wslot[fs % 4][:, c * 128:(c + 1) * 128],
                                       xln[c][:, tt * TT:(tt + 1) * TT],
                                       start=(c == 0), stop=(c == DC - 1))
                        if c == DC - 1:
                            p.inc(ins, "sPE")

        @block.scalar
        def _(s):
            for fs in range(I0C):
                for tt in range(NT):
                    k = fs * NT + tt
                    p.w(s, "sPE", b_pe + k + 1)
                    if fs >= 2 and tt == 0:
                        p.w(s, "sGa" if fs % 2 == 0 else "sGb", b_gp[fs % 2] + 16 * (fs // 2))
                        p.w(s, "sDV", b_dv + 1 + (fs - 1))
                    ins = s.activation(out=stg[fs % 2][:, tt * TT:(tt + 1) * TT],
                                       in_=ps[k % NB][:, :], func=AF.Copy, scale=scl[:, 0:1])
                    p.inc(ins, "sAC")

        @block.gpsimd
        def _(g):
            for fs in range(I0C):
                p.w(g, "sAC", b_ac + fs * NT + NT)
                p.inc(g.dma_start(out=spill[fs * 128:(fs + 1) * 128, :], in_=stg[fs % 2][:, :]),
                      "sGa" if fs % 2 == 0 else "sGb", 16)


def _stream_g2(ctx, sp_gate, sp_g2):
    p, nc = ctx["p"], ctx["nc"]
    acc2, iti, sqt, scl, maxac = ctx["acc2"], ctx["iti"], ctx["sqt"], ctx["scl"], ctx["maxac"]
    iti2 = ctx["iti2"]
    gin = [acc2[:, 0:T], acc2[:, T:2 * T]]
    gout = [acc2[:, 2 * T:3 * T], acc2[:, 3 * T:4 * T]]

    with nc.Block() as block:
        b_sp = [p.cnt["sSa"], p.cnt["sSb"]]
        b_gp = [p.cnt["sGa"], p.cnt["sGb"]]
        b_dv = p.cnt["sDV"]
        b_ac = p.cnt["sAC"]
        NV = 5

        @block.sync
        def _(sync):
            p.fence(sync)
            for fs in range(I0C):
                if fs >= 2:
                    p.w(sync, "sDV", b_dv + 1 + NV * ((fs - 1) * NT))
                p.inc(sync.dma_start(out=gin[fs % 2][:, :], in_=sp_gate[fs * 128:(fs + 1) * 128, :]),
                      "sSa" if fs % 2 == 0 else "sSb", 16)

        @block.vector
        def _(v):
            i = v.memset(ctx["mxc"][:, :], 0.0)
            p.w(v, "sDV", p.inc(i, "sDV"))
            for fs in range(I0C):
                p.w(v, "sSa" if fs % 2 == 0 else "sSb", b_sp[fs % 2] + 16 * (fs // 2 + 1))
                if fs >= 2:
                    p.w(v, "sGa" if fs % 2 == 0 else "sGb", b_gp[fs % 2] + 16 * (fs // 2))
                for tt in range(NT):
                    k = fs * NT + tt
                    j = k % 2
                    i0 = v.tensor_scalar(out=iti[j][:, :], in0=gin[fs % 2][:, tt * TT:(tt + 1) * TT],
                                         scalar1=scl[:, 5:6], scalar2=None, op0=ALU.mult)
                    c0 = p.inc(i0, "sDV")
                    p.w(v, "sAC", b_ac + k + 1)
                    i1 = v.tensor_scalar(out=ctx["iti2"][j][:, :], in0=sqt[j][:, :],
                                         scalar1=65536.0, scalar2=-32768.0, op0=ALU.mult, op1=ALU.add)
                    p.w(v, "sDV", p.inc(i1, "sDV"))
                    i2 = v.tensor_scalar(out=sqt[j][:, :], in0=ctx["iti2"][j][:, :],
                                         scalar1=1.0 / 65536.0, scalar2=0.5, op0=ALU.mult, op1=ALU.add)
                    p.w(v, "sDV", p.inc(i2, "sDV"))
                    i3 = v.scalar_tensor_tensor(out=gout[fs % 2][:, tt * TT:(tt + 1) * TT],
                                                in0=iti[j][:, :], scalar=scl[:, 4:5], in1=sqt[j][:, :],
                                                op0=ALU.mult, op1=ALU.mult)
                    p.w(v, "sDV", p.inc(i3, "sDV"))
                    if tt == NT - 1:
                        i4 = v.tensor_reduce(out=ctx["mxc"][:, fs % 32:fs % 32 + 1],
                                             in_=gout[fs % 2][:, :], axis=AX.X, op=ALU.max,
                                             apply_absolute_value=True)
                        p.w(v, "sDV", p.inc(i4, "sDV"))
                    else:
                        p.inc(v.engine_nop(), "sDV")
                        p.w(v, "sDV")
            i5 = v.tensor_reduce(out=maxac[:, 3:4], in_=ctx["mxc"][:, :], axis=AX.X, op=ALU.max)
            p.w(v, "sDV", p.inc(i5, "sDV"))

        @block.scalar
        def _(s):
            for fs in range(I0C):
                for tt in range(NT):
                    k = fs * NT + tt
                    j = k % 2
                    p.w(s, "sDV", b_dv + 1 + NV * k + 1)
                    ins = s.activation(out=sqt[j][:, :], in_=iti[j][:, :],
                                       func=AF.Sigmoid, scale=scl[:, 4:5])
                    p.inc(ins, "sAC")

        @block.tensor
        def _(t):
            od, pm = ctx["onesd"], ctx["psm"]
            for k in range(I0C * NT):
                p.w(t, "sDV", b_dv + 1 + NV * k)
                t.matmul(pm[:, :], od[:, :], od[:, :], start=True, stop=True)

        @block.gpsimd
        def _(g):
            for fs in range(I0C):
                p.w(g, "sDV", b_dv + 1 + NV * (fs * NT + NT))
                p.inc(g.dma_start(out=sp_g2[fs * 128:(fs + 1) * 128, :], in_=gout[fs % 2][:, :]),
                      "sGa" if fs % 2 == 0 else "sGb", 16)


def _stream_o(ctx, sp_g2, sp_up, sp_o):
    p, nc = ctx["p"], ctx["nc"]
    acc2, iti, iti2, scl, maxac = ctx["acc2"], ctx["iti"], ctx["iti2"], ctx["scl"], ctx["maxac"]
    ain = [acc2[:, 0:T], acc2[:, T:2 * T]]
    bin_ = [acc2[:, 2 * T:3 * T], acc2[:, 3 * T:4 * T]]
    oout = [acc2[:, 4 * T:5 * T], acc2[:, 5 * T:6 * T]]

    with nc.Block() as block:
        b_sp = [p.cnt["sSa"], p.cnt["sSb"]]
        b_gp = [p.cnt["sGa"], p.cnt["sGb"]]
        b_dv = p.cnt["sDV"]
        NV = 4

        @block.sync
        def _(sync):
            p.fence(sync)
            for fs in range(I0C):
                if fs >= 2:
                    p.w(sync, "sDV", b_dv + 1 + NV * ((fs - 1) * NT))
                sem = "sSa" if fs % 2 == 0 else "sSb"
                p.inc(sync.dma_start(out=ain[fs % 2][:, :], in_=sp_g2[fs * 128:(fs + 1) * 128, :]), sem, 16)
                p.inc(sync.dma_start(out=bin_[fs % 2][:, :], in_=sp_up[fs * 128:(fs + 1) * 128, :]), sem, 16)

        @block.vector
        def _(v):
            i = v.memset(ctx["mxc"][:, :], 0.0)
            p.w(v, "sDV", p.inc(i, "sDV"))
            for fs in range(I0C):
                p.w(v, "sSa" if fs % 2 == 0 else "sSb", b_sp[fs % 2] + 32 * (fs // 2 + 1))
                if fs >= 2:
                    p.w(v, "sGa" if fs % 2 == 0 else "sGb", b_gp[fs % 2] + 16 * (fs // 2))
                for tt in range(NT):
                    k = fs * NT + tt
                    j = k % 2
                    i0 = v.tensor_scalar(out=iti[j][:, :], in0=ain[fs % 2][:, tt * TT:(tt + 1) * TT],
                                         scalar1=scl[:, 7:8], scalar2=None, op0=ALU.mult)
                    p.w(v, "sDV", p.inc(i0, "sDV"))
                    i1 = v.tensor_scalar(out=iti2[j][:, :], in0=bin_[fs % 2][:, tt * TT:(tt + 1) * TT],
                                         scalar1=scl[:, 3:4], scalar2=None, op0=ALU.mult)
                    p.w(v, "sDV", p.inc(i1, "sDV"))
                    i2 = v.scalar_tensor_tensor(out=oout[fs % 2][:, tt * TT:(tt + 1) * TT],
                                                in0=iti[j][:, :], scalar=scl[:, 8:9], in1=iti2[j][:, :],
                                                op0=ALU.mult, op1=ALU.mult)
                    p.w(v, "sDV", p.inc(i2, "sDV"))
                    if tt == NT - 1:
                        i4 = v.tensor_reduce(out=ctx["mxc"][:, fs % 32:fs % 32 + 1],
                                             in_=oout[fs % 2][:, :], axis=AX.X, op=ALU.max,
                                             apply_absolute_value=True)
                        p.w(v, "sDV", p.inc(i4, "sDV"))
                    else:
                        p.inc(v.engine_nop(), "sDV")
                        p.w(v, "sDV")
            i5 = v.tensor_reduce(out=maxac[:, 4:5], in_=ctx["mxc"][:, :], axis=AX.X, op=ALU.max)
            p.w(v, "sDV", p.inc(i5, "sDV"))

        @block.gpsimd
        def _(g):
            for fs in range(I0C):
                p.w(g, "sDV", b_dv + 1 + NV * (fs * NT + NT))
                p.inc(g.dma_start(out=sp_o[fs * 128:(fs + 1) * 128, :], in_=oout[fs % 2][:, :]),
                      "sGa" if fs % 2 == 0 else "sGb", 16)

        @block.tensor
        def _(t):
            od, pm = ctx["onesd"], ctx["psm"]
            for k in range(I0C * NT):
                p.w(t, "sDV", b_dv + 1 + NV * k)
                t.matmul(pm[:, :], od[:, :], od[:, :], start=True, stop=True)


def _down_mm(ctx, sp_o):
    p, nc = ctx["p"], ctx["nc"]
    acc2, iti, scl, wst, ps, x_res = ctx["acc2"], ctx["iti"], ctx["scl"], ctx["wst"], ctx["ps"], ctx["x_res"]
    s_downT = ctx["s_downT"]
    oslot = [acc2[:, fs * TT:(fs + 1) * TT] for fs in range(I0C)]
    wsem = ["sSc", "sSd", "sSe", "sSf"]
    wslot = [wst[0][:, 0:I0], wst[1][:, 0:I0],
             ctx["abuf"][:, 0:I0], ctx["abuf"][:, 4096:4096 + I0]]
    osem = ["sSa", "sSb", "sSg", "sSh"]
    oload = [ctx["sqt"][0], ctx["sqt"][1],
             ctx["abuf"][:, 8192:8192 + TT], ctx["abuf"][:, 8192 + TT:8192 + 2 * TT]]
    with nc.Block() as block:
        b_so = [p.cnt[s] for s in osem]
        b_sw = [p.cnt[s] for s in wsem]
        b_pe = p.cnt["sPE"]
        b_dv = p.cnt["sDV"]
        DVT = 2 * I0C + DC
        PET = DC

        @block.sync
        def _(sync):
            p.fence(sync)
            for tt in range(NT):
                for fs in range(I0C):
                    gi = tt * I0C + fs
                    if gi >= 4:
                        p.w(sync, "sDV", b_dv + DVT * ((gi - 4) // I0C) + 2 * ((gi - 4) % I0C) + 1)
                    p.inc(sync.dma_start(out=oload[gi % 4][:, :],
                                         in_=sp_o[fs * 128:(fs + 1) * 128, tt * TT:(tt + 1) * TT]),
                          osem[gi % 4], 16)
                for m in range(DC):
                    k = tt * DC + m
                    if k >= 4:
                        p.w(sync, "sPE", b_pe + k - 3)
                    p.inc(sync.dma_start(
                        out=wslot[k % 4],
                        in_=AP(s_downT, m * 128, [[D, 128], [128 * D, I0C], [1, 128]])),
                        wsem[k % 4], 16)

        @block.vector
        def _(v):
            for tt in range(NT):
                for fs in range(I0C):
                    gi = tt * I0C + fs
                    if tt > 0 and fs == 0:
                        p.w(v, "sPE", b_pe + PET * tt)
                    p.w(v, osem[gi % 4], b_so[gi % 4] + 16 * (gi // 4 + 1))
                    i0 = v.tensor_scalar(out=iti[gi % 2][:, :], in0=oload[gi % 4][:, :],
                                         scalar1=scl[:, 10:11], scalar2=None, op0=ALU.mult)
                    p.w(v, "sDV", p.inc(i0, "sDV"))
                    i1 = v.tensor_scalar(out=oslot[fs][:, :], in0=iti[gi % 2][:, :],
                                         scalar1=1.0, scalar2=None, op0=ALU.mult)
                    p.w(v, "sDV", p.inc(i1, "sDV"))
                for m in range(DC):
                    p.w(v, "sPE", b_pe + PET * tt + m + 1)
                    ins = v.scalar_tensor_tensor(out=x_res[m][:, tt * TT:(tt + 1) * TT],
                                                 in0=ps[m % 4][:, :], scalar=scl[:, 11:12],
                                                 in1=x_res[m][:, tt * TT:(tt + 1) * TT],
                                                 op0=ALU.mult, op1=ALU.add)
                    p.inc(ins, "sDV")

        @block.tensor
        def _(t):
            od, pm = ctx["onesd"], ctx["psm"]
            for tt in range(NT):
                for kk in range(I0C):
                    p.w(t, "sDV", b_dv + DVT * tt + 2 * kk + 1)
                    t.matmul(pm[:, :], od[:, :], od[:, :], start=True, stop=True)
                for m in range(DC):
                    k = tt * DC + m
                    p.w(t, wsem[k % 4], b_sw[k % 4] + 16 * (k // 4 + 1))
                    p.w(t, "sDV", b_dv + DVT * tt + 2 * I0C)
                    if k >= 4:
                        p.w(t, "sDV", b_dv + DVT * ((k - 4) // DC) + 2 * I0C + (k - 4) % DC + 1)
                    for fs in range(I0C):
                        ins = t.matmul(ps[m % 4][:, :], wslot[k % 4][:, fs * 128:(fs + 1) * 128],
                                       oslot[fs][:, :], start=(fs == 0), stop=(fs == I0C - 1))
                        if fs == I0C - 1:
                            p.inc(ins, "sPE")


def _attn_l0(ctx):
    p, nc = ctx["p"], ctx["nc"]
    x_res, xln, abuf, acc2 = ctx["x_res"], ctx["xln"], ctx["abuf"], ctx["acc2"]
    wst, sqt, ps, tmpb = ctx["wst"], ctx["sqt"], ctx["ps"], ctx["tmpb"]
    tmpa = ctx["tmpa"]
    d_k0, d_wo0 = ctx["d_k0"], ctx["d_wo0"]
    onesr, constt = ctx["onesr"], ctx["constt"]
    VS = 1040
    QOF, KOF = 16640, 18688

    _layernorm(ctx)

    with nc.Block() as block:
        @block.gpsimd
        def _(g):
            for c in range(DC):
                p.inc(g.dma_start(out=wst[c // 4][:, (c % 4) * 1024:(c % 4 + 1) * 1024],
                                  in_=d_k0[2][c * 128:(c + 1) * 128, :]), "sG", 16)
            g.memset(AP(abuf, 64, [[20736, 128], [1040, 16], [65, 16], [1, 1]]), 1.0)

    with nc.Block() as block:
        b_pe = p.cnt["sPE"]
        b_ac = p.cnt["sAC"]

        @block.tensor
        def _(t):
            p.fence(t)
            for kt in range(NCH):
                for gh in range(2):
                    k = kt * 2 + gh
                    if k >= 4:
                        p.w(t, "sAC", b_ac + k - 3)
                    for c in range(DC):
                        ins = t.matmul(ps[k % 4][:, :],
                                       xln[c][:, kt * 128:(kt + 1) * 128],
                                       wst[c // 4][:, (c % 4) * 1024 + gh * 512:(c % 4) * 1024 + (gh + 1) * 512],
                                       start=(c == 0), stop=(c == DC - 1))
                        if c == DC - 1:
                            p.inc(ins, "sPE")

        @block.scalar
        def _(s):
            for kt in range(NCH):
                for gh in range(2):
                    k = kt * 2 + gh
                    p.w(s, "sPE", b_pe + k + 1)
                    ins = s.activation(
                        out=AP(abuf, kt * VS + gh * 8 * 65, [[20736, 128], [65, 8], [1, 64]]),
                        in_=ps[k % 4][:, :], func=AF.Copy)
                    p.inc(ins, "sAC")

    PEJ, ACJ, DVJ = 264, 136, 16
    with nc.Block() as block:
        b_pe = p.cnt["sPE"]
        b_ac = p.cnt["sAC"]
        b_dv = p.cnt["sDV"]
        b_sg = p.cnt["sG"]
        b_gp = [p.cnt["sGa"], p.cnt["sGb"]]

        @block.gpsimd
        def _(g):
            for j in range(DC):
                if j >= 2:
                    p.w(g, "sPE", b_pe + PEJ * (j - 2) + 8)
                wsl = wst[j % 2]
                gs = "sGa" if j % 2 == 0 else "sGb"
                p.inc(g.dma_start(out=wsl[:, 0:1024],
                                  in_=AP(d_k0[0], j * 128, [[1024, 128], [131072, 8], [1, 128]])), gs, 16)
                p.inc(g.dma_start(out=wsl[:, 1024:2048],
                                  in_=AP(d_k0[1], j * 128, [[1024, 128], [131072, 8], [1, 128]])), gs, 16)
                for hh in range(2):
                    h = 2 * j + hh
                    p.w(g, "sDV", b_dv + DVJ * j + hh * 8 + 8)
                    p.inc(g.dma_start(out=ctx["d_dend"][h, :],
                                      in_=AP(tmpb, 0, [[32 * 1024, 2], [1, 1024]])), "sG", 16)

        @block.tensor
        def _(t):
            for j in range(DC):
                bp, ba, bd = b_pe + PEJ * j, b_ac + ACJ * j, b_dv + DVJ * j
                p.w(t, "sGa" if j % 2 == 0 else "sGb", b_gp[j % 2] + 32 * (j // 2 + 1))
                for qk in range(2):
                    for tt in range(NT):
                        k = qk * NT + tt
                        if k >= 4:
                            p.w(t, "sAC", ba + k - 3)
                        elif j >= 1:
                            p.w(t, "sDV", bd - DVJ + 8 + 2 * k + 2)
                        for c in range(DC):
                            ins = t.matmul(ps[k % 4][:, :],
                                           wst[j % 2][:, qk * 1024 + c * 128:qk * 1024 + (c + 1) * 128],
                                           xln[c][:, tt * TT:(tt + 1) * TT],
                                           start=(c == 0), stop=(c == DC - 1))
                            if c == DC - 1:
                                p.inc(ins, "sPE")
                for hh in range(2):
                    h = 2 * j + hh
                    r0, r1 = hh * 64, hh * 64 + 64
                    for kt in range(NCH):
                        for tp in range(2):
                            k2 = kt * 2 + tp
                            for half in range(2):
                                tt = 2 * tp + half
                                p.w(t, "sAC", max(ba + 8,
                                                  ba + 8 + hh * 64 + 2 * k2 + half - 1))
                                ins = t.matmul(ps[4 + half][:, :],
                                               abuf[r0:r1, KOF + kt * 128:KOF + (kt + 1) * 128],
                                               abuf[r0:r1, QOF + tt * TT:QOF + (tt + 1) * TT],
                                               start=True, stop=True,
                                               tile_position=(r0, 0))
                                p.inc(ins, "sPE")
                            for half in range(2):
                                tt = 2 * tp + half
                                p.w(t, "sAC", ba + 8 + hh * 64 + 2 * k2 + half + 1)
                                if hh == 1 and kt == 0:
                                    p.w(t, "sDV", bd + 2 * tt + 2)
                                ins2 = t.matmul(ps[tt][0:65, :],
                                                abuf[:, kt * VS + h * 65:kt * VS + (h + 1) * 65],
                                                ctx["sq2"][:, half * TT:(half + 1) * TT],
                                                start=(kt == 0), stop=(kt == NCH - 1))
                                p.inc(ins2, "sPE")

        @block.scalar
        def _(s):
            for j in range(DC):
                bp, ba = b_pe + PEJ * j, b_ac + ACJ * j
                for qk in range(2):
                    for tt in range(NT):
                        k = qk * NT + tt
                        p.w(s, "sPE", bp + k + 1)
                        off = QOF if qk == 0 else KOF
                        ins = s.activation(out=abuf[:, off + tt * TT:off + (tt + 1) * TT],
                                           in_=ps[k % 4][:, :], func=AF.Copy)
                        p.inc(ins, "sAC")
                for hh in range(2):
                    for kt in range(NCH):
                        for tp in range(2):
                            k2 = kt * 2 + tp
                            for half in range(2):
                                p.w(s, "sPE", bp + 8 + hh * 128 + 4 * k2 + half + 1)
                                ins = s.activation(
                                    out=ctx["sq2"][:, half * TT:(half + 1) * TT],
                                    in_=ctx["psW"][:, half * TT:(half + 1) * TT],
                                    func=AF.Exp, scale=constt[:, 2:3])
                                p.inc(ins, "sAC")

        @block.vector
        def _(v):
            for j in range(DC):
                bp = b_pe + PEJ * j
                for hh in range(2):
                    r0, r1 = hh * 64, hh * 64 + 64
                    p.w(v, "sPE", bp + 8 + (hh + 1) * 128)
                    p.w(v, "sG", b_sg + 16 * (2 * j + hh))
                    for tt in range(NT):
                        i0 = v.tensor_scalar(out=acc2[r0:r1, j * T + tt * TT:j * T + (tt + 1) * TT],
                                             in0=ps[tt][0:64, :], scalar1=1.0, scalar2=None,
                                             op0=ALU.mult)
                        p.w(v, "sDV", p.inc(i0, "sDV"))
                        i1 = v.tensor_scalar(out=tmpb[(tt // 2) * 32:(tt // 2) * 32 + 1,
                                                      (tt % 2) * TT:(tt % 2 + 1) * TT],
                                             in0=ps[tt][64:65, :], scalar1=1.0, scalar2=None,
                                             op0=ALU.mult)
                        p.w(v, "sDV", p.inc(i1, "sDV"))

    with nc.Block() as block:
        @block.sync
        def _(sync):
            p.fence(sync)
            p.inc(sync.dma_start(out=tmpa[:, 0:256],
                                 in_=AP(ctx["d_dend"], 0, [[256, 128], [1, 256]])), "sS", 16)
            p.w(sync, "sS")

        @block.vector
        def _(v):
            p.w(v, "sS")
            i0 = v.reciprocal(out=tmpb[:, 0:256], in_=tmpa[:, 0:256])
            p.w(v, "sDV", p.inc(i0, "sDV"))

        @block.gpsimd
        def _(g):
            p.w(g, "sDV", p.cnt["sDV"])
            p.inc(g.dma_start(out=AP(ctx["d_rdend"], 0, [[256, 128], [1, 256]]),
                              in_=tmpb[:, 0:256]), "sG", 16)
            p.w(g, "sG")

    with nc.Block() as block:
        b_gp = [p.cnt["sGa"], p.cnt["sGb"]]
        b_pe = p.cnt["sPE"]
        b_ac = p.cnt["sAC"]
        b_dv = p.cnt["sDV"]

        @block.gpsimd
        def _(g):
            p.fence(g)
            for idx in range(NHEAD * NT):
                hh, tt = divmod(idx, NT)
                if idx >= 1:
                    p.w(g, "sPE", b_pe + idx)
                p.inc(g.dma_start(out=ctx["sqt"][0][0:1, :],
                                  in_=ctx["d_rdend"][hh, tt * TT:(tt + 1) * TT]),
                      "sGa" if idx % 2 == 0 else "sGb", 16)

        @block.tensor
        def _(t):
            for idx in range(NHEAD * NT):
                p.w(t, "sGa" if idx % 2 == 0 else "sGb", b_gp[idx % 2] + 16 * (idx // 2 + 1))
                if idx >= 2:
                    p.w(t, "sAC", b_ac + idx - 1)
                ins = t.matmul(ps[4 + (idx % 2)][:, :], onesr[0:1, :], ctx["sqt"][0][0:1, :],
                               start=True, stop=True)
                p.inc(ins, "sPE")

        @block.scalar
        def _(s):
            for idx in range(NHEAD * NT):
                hh, tt = divmod(idx, NT)
                rr0 = (hh % 2) * 64
                p.w(s, "sPE", b_pe + idx + 1)
                if idx >= 1:
                    p.w(s, "sDV", b_dv + idx)
                ins = s.activation(out=ctx["sq2"][rr0:rr0 + 64, TT:2 * TT],
                                   in_=ps[4 + (idx % 2)][rr0:rr0 + 64, :], func=AF.Copy)
                p.inc(ins, "sAC")

        @block.vector
        def _(v):
            for idx in range(NHEAD * NT):
                hh, tt = divmod(idx, NT)
                j2, h2 = divmod(hh, 2)
                rr0 = h2 * 64
                p.w(v, "sAC", b_ac + idx + 1)
                ins = v.tensor_tensor(out=acc2[rr0:rr0 + 64, j2 * T + tt * TT:j2 * T + (tt + 1) * TT],
                                      in0=acc2[rr0:rr0 + 64, j2 * T + tt * TT:j2 * T + (tt + 1) * TT],
                                      in1=ctx["sq2"][rr0:rr0 + 64, TT:2 * TT], op=ALU.mult)
                p.inc(ins, "sDV")
    
    with nc.Block() as block:
        b_gp = [p.cnt["sGa"], p.cnt["sGb"]]
        b_pe = p.cnt["sPE"]
        b_dv = p.cnt["sDV"]

        @block.gpsimd
        def _(g):
            for m in range(DC):
                if m >= 2:
                    p.w(g, "sPE", b_pe + (m - 1) * NT)
                p.inc(g.dma_start(out=wst[m % 2][:, 0:1024],
                                  in_=AP(d_wo0, m * 128, [[1024, 128], [131072, 8], [1, 128]])),
                      "sGa" if m % 2 == 0 else "sGb", 16)

        @block.tensor
        def _(t):
            for m in range(DC):
                p.w(t, "sGa" if m % 2 == 0 else "sGb", b_gp[m % 2] + 16 * (m // 2 + 1))
                for tt in range(NT):
                    k = m * NT + tt
                    if k >= 4:
                        p.w(t, "sDV", b_dv + k - 3)
                    for gc in range(DC):
                        ins = t.matmul(ps[k % 4][:, :], wst[m % 2][:, gc * 128:(gc + 1) * 128],
                                       acc2[:, gc * T + tt * TT:gc * T + (tt + 1) * TT],
                                       start=(gc == 0), stop=(gc == DC - 1))
                        if gc == DC - 1:
                            p.inc(ins, "sPE")

        @block.vector
        def _(v):
            for m in range(DC):
                for tt in range(NT):
                    k = m * NT + tt
                    p.w(v, "sPE", b_pe + k + 1)
                    ins = v.tensor_add(out=x_res[m][:, tt * TT:(tt + 1) * TT],
                                       in0=ps[k % 4][:, :],
                                       in1=x_res[m][:, tt * TT:(tt + 1) * TT])
                    p.inc(ins, "sDV")


def _ffn_l1(ctx, stop_after_h=False):
    p, nc = ctx["p"], ctx["nc"]
    x_res, xln, iti, scl, maxac = ctx["x_res"], ctx["xln"], ctx["iti"], ctx["scl"], ctx["maxac"]
    acc2, wst, sqt, ps, stg = ctx["acc2"], ctx["wst"], ctx["sqt"], ctx["ps"], ctx["stg"]
    tmpa, tmpb = ctx["tmpa"], ctx["tmpb"]
    sw1, sw2, b1t, b2t, b1q, b2q = ctx["sw1"], ctx["sw2"], ctx["b1t"], ctx["b2t"], ctx["b1q"], ctx["b2q"]
    s_w1T, s_w2T, sp_h = ctx["s_w1T"], ctx["s_w2T"], ctx["sp_h"]
    MAGIC = 12582912.0

    _layernorm(ctx)

    with nc.Block() as block:
        @block.vector
        def _(v):
            i = v.memset(maxac[:, 0:1], -3.0e38)
            p.w(v, "sDV", p.inc(i, "sDV"))
            _absmax_tiles(ctx, v, [xln[c][:, :] for c in range(DC)], 0, signed_max=True)
    _collective_max(ctx, [0])
    with nc.Block() as block:
        @block.vector
        def _(v):
            i0 = v.tensor_scalar(out=scl[:, 0:1], in0=ctx["gbc"][:, 0:1],
                                 scalar1=1e-8, scalar2=1.0 / 255.0, op0=ALU.max, op1=ALU.mult)
            p.w(v, "sDV", p.inc(i0, "sDV"))
            i1 = v.reciprocal(out=scl[:, 1:2], in_=scl[:, 0:1])
            p.w(v, "sDV", p.inc(i1, "sDV"))
            i2 = v.tensor_scalar(out=tmpa[:, 0:F1C], in0=sw1[:, :], scalar1=scl[:, 0:1],
                                 scalar2=None, op0=ALU.mult)
            p.w(v, "sDV", p.inc(i2, "sDV"))
            i3 = v.reciprocal(out=tmpa[:, F1C:2 * F1C], in_=tmpa[:, 0:F1C])
            p.w(v, "sDV", p.inc(i3, "sDV"))
            i4 = v.tensor_tensor(out=tmpb[:, 0:F1C], in0=b1t[:, :], in1=tmpa[:, F1C:2 * F1C], op=ALU.mult)
            p.w(v, "sDV", p.inc(i4, "sDV"))
            i5 = v.tensor_scalar(out=tmpb[:, 0:F1C], in0=tmpb[:, 0:F1C],
                                 scalar1=MAGIC, scalar2=-MAGIC, op0=ALU.add, op1=ALU.add)
            p.w(v, "sDV", p.inc(i5, "sDV"))
            i6 = v.tensor_tensor(out=b1q[:, :], in0=tmpb[:, 0:F1C], in1=tmpa[:, 0:F1C], op=ALU.mult)
            p.w(v, "sDV", p.inc(i6, "sDV"))

    with nc.Block() as block:
        b_dvq = p.cnt["sDV"]

        @block.vector
        def _(v):
            for c in range(DC):
                for tt in range(NT):
                    j = (c * NT + tt) % 2
                    sl = xln[c][:, tt * TT:(tt + 1) * TT]
                    i0 = v.tensor_scalar(out=iti[j][:, :], in0=sl, scalar1=scl[:, 1:2],
                                         scalar2=0.0, op0=ALU.mult, op1=ALU.max)
                    p.w(v, "sDV", p.inc(i0, "sDV"))
                    i1 = v.tensor_scalar(out=sl, in0=iti[j][:, :], scalar1=1.0,
                                         scalar2=None, op0=ALU.mult)
                    p.w(v, "sDV", p.inc(i1, "sDV"))

        @block.tensor
        def _(t):
            od, pm = ctx["onesd"], ctx["psm"]
            for k in range(DC * NT):
                p.w(t, "sDV", b_dvq + 2 * k + 1)
                t.matmul(pm[:, :], od[:, :], od[:, :], start=True, stop=True)

    wsem = ["sSa", "sSb", "sSc", "sSd"]
    wslot = [wst[0][:, 0:1024], wst[0][:, 1024:2048],
             wst[1][:, 0:1024], wst[1][:, 1024:2048]]
    with nc.Block() as block:
        b_sp = [p.cnt[s] for s in wsem]
        b_gp = [p.cnt["sGa"], p.cnt["sGb"]]
        b_pe = p.cnt["sPE"]
        b_ac = p.cnt["sAC"]
        b_dv = p.cnt["sDV"]
        NB = 6

        mxc = ctx["mxc"]

        @block.vector
        def _(v):
            i = v.memset(mxc[:, :], 0.0)
            p.w(v, "sDV", p.inc(i, "sDV"))
            for fs in range(F1C):
                p.w(v, "sAC", b_ac + fs * NT + NT)
                i0 = v.tensor_reduce(out=mxc[:, fs % 32:fs % 32 + 1], in_=stg[fs % 2][:, :],
                                     axis=AX.X, op=ALU.max, apply_absolute_value=True)
                p.inc(i0, "sDV")
            p.w(v, "sDV")
            i1 = v.tensor_reduce(out=maxac[:, 1:2], in_=mxc[:, :], axis=AX.X, op=ALU.max)
            p.w(v, "sDV", p.inc(i1, "sDV"))

        @block.sync
        def _(sync):
            p.fence(sync)
            for fs in range(F1C):
                if fs >= 4:
                    p.w(sync, "sPE", b_pe + (fs - 3) * NT)
                p.inc(sync.dma_start(
                    out=wslot[fs % 4],
                    in_=AP(s_w1T, fs * 128, [[F1, 128], [128 * F1, DC], [1, 128]])),
                    wsem[fs % 4], 16)

        @block.tensor
        def _(t):
            for fs in range(F1C):
                p.w(t, wsem[fs % 4], b_sp[fs % 4] + 16 * (fs // 4 + 1))
                for tt in range(NT):
                    k = fs * NT + tt
                    if k >= NB:
                        p.w(t, "sAC", b_ac + (k - NB) + 1)
                    for c in range(DC):
                        ins = t.matmul(ps[k % NB][:, :], wslot[fs % 4][:, c * 128:(c + 1) * 128],
                                       xln[c][:, tt * TT:(tt + 1) * TT],
                                       start=(c == 0), stop=(c == DC - 1))
                        if c == DC - 1:
                            p.inc(ins, "sPE")

        @block.scalar
        def _(s):
            for fs in range(F1C):
                for tt in range(NT):
                    k = fs * NT + tt
                    p.w(s, "sPE", b_pe + k + 1)
                    if fs >= 2 and tt == 0:
                        p.w(s, "sGa" if fs % 2 == 0 else "sGb", b_gp[fs % 2] + 16 * (fs // 2))
                        p.w(s, "sDV", b_dv + 1 + (fs - 1))
                    ins = s.activation(out=stg[fs % 2][:, tt * TT:(tt + 1) * TT],
                                       in_=ps[k % NB][:, :], func=AF.Relu,
                                       scale=tmpa[:, fs:fs + 1], bias=b1q[:, fs:fs + 1])
                    p.inc(ins, "sAC")

        @block.gpsimd
        def _(g):
            for fs in range(F1C):
                p.w(g, "sAC", b_ac + fs * NT + NT)
                p.inc(g.dma_start(out=sp_h[fs * 128:(fs + 1) * 128, :], in_=stg[fs % 2][:, :]),
                      "sGa" if fs % 2 == 0 else "sGb", 16)

    if stop_after_h:
        return
    _collective_max(ctx, [1])
    with nc.Block() as block:
        @block.vector
        def _(v):
            i0 = v.tensor_scalar(out=scl[:, 2:3], in0=ctx["gbc"][:, 1:2],
                                 scalar1=1e-8, scalar2=1.0 / 255.0, op0=ALU.max, op1=ALU.mult)
            p.w(v, "sDV", p.inc(i0, "sDV"))
            i1 = v.reciprocal(out=scl[:, 3:4], in_=scl[:, 2:3])
            p.w(v, "sDV", p.inc(i1, "sDV"))
            i2 = v.tensor_scalar(out=tmpa[:, 40:40 + DC], in0=sw2[:, :], scalar1=scl[:, 2:3],
                                 scalar2=None, op0=ALU.mult)
            p.w(v, "sDV", p.inc(i2, "sDV"))
            i3 = v.reciprocal(out=tmpa[:, 48:48 + DC], in_=tmpa[:, 40:40 + DC])
            p.w(v, "sDV", p.inc(i3, "sDV"))
            i4 = v.tensor_tensor(out=tmpb[:, 0:DC], in0=b2t[:, :], in1=tmpa[:, 48:48 + DC], op=ALU.mult)
            p.w(v, "sDV", p.inc(i4, "sDV"))
            i5 = v.tensor_scalar(out=tmpb[:, 0:DC], in0=tmpb[:, 0:DC],
                                 scalar1=MAGIC, scalar2=-MAGIC, op0=ALU.add, op1=ALU.add)
            p.w(v, "sDV", p.inc(i5, "sDV"))
            i6 = v.tensor_tensor(out=b2q[:, :], in0=tmpb[:, 0:DC], in1=tmpa[:, 40:40 + DC], op=ALU.mult)
            p.w(v, "sDV", p.inc(i6, "sDV"))
            i7 = v.tensor_scalar(out=b2q[:, :], in0=b2q[:, :], scalar1=0.5, scalar2=None, op0=ALU.mult)
            p.w(v, "sDV", p.inc(i7, "sDV"))
            i8 = v.tensor_scalar(out=tmpa[:, 40:40 + DC], in0=tmpa[:, 40:40 + DC],
                                 scalar1=0.5, scalar2=None, op0=ALU.mult)
            p.w(v, "sDV", p.inc(i8, "sDV"))

    oslot2 = [acc2[:, fs * TT:(fs + 1) * TT] for fs in range(F1C)]
    wsem = ["sSc", "sSd", "sSe", "sSf"]
    wslot = [wst[0][:, 0:F1], wst[1][:, 0:F1],
             ctx["abuf"][:, 0:F1], ctx["abuf"][:, 4096:4096 + F1]]
    osem = ["sSa", "sSb", "sSg", "sSh"]
    oload = [ctx["sqt"][0], ctx["sqt"][1],
             ctx["abuf"][:, 8192:8192 + TT], ctx["abuf"][:, 8192 + TT:8192 + 2 * TT]]
    with nc.Block() as block:
        b_so = [p.cnt[s] for s in osem]
        b_sw = [p.cnt[s] for s in wsem]
        b_pe = p.cnt["sPE"]
        b_ac = p.cnt["sAC"]
        b_dv = p.cnt["sDV"]
        DVT = 2 * F1C + DC
        PET = DC

        @block.sync
        def _(sync):
            p.fence(sync)
            for tt in range(NT):
                for fs in range(F1C):
                    gi = tt * F1C + fs
                    if gi >= 4:
                        p.w(sync, "sDV", b_dv + DVT * ((gi - 4) // F1C) + 2 * ((gi - 4) % F1C) + 1)
                    p.inc(sync.dma_start(out=oload[gi % 4][:, :],
                                         in_=sp_h[fs * 128:(fs + 1) * 128, tt * TT:(tt + 1) * TT]),
                          osem[gi % 4], 16)
                for m in range(DC):
                    k = tt * DC + m
                    if k >= 4:
                        p.w(sync, "sPE", b_pe + k - 3)
                    p.inc(sync.dma_start(
                        out=wslot[k % 4],
                        in_=AP(s_w2T, m * 128, [[D, 128], [128 * D, F1C], [1, 128]])),
                        wsem[k % 4], 16)

        @block.vector
        def _(v):
            for tt in range(NT):
                for fs in range(F1C):
                    gi = tt * F1C + fs
                    if tt > 0 and fs == 0:
                        p.w(v, "sPE", b_pe + PET * tt)
                    p.w(v, osem[gi % 4], b_so[gi % 4] + 16 * (gi // 4 + 1))
                    i0 = v.tensor_scalar(out=iti[gi % 2][:, :], in0=oload[gi % 4][:, :],
                                         scalar1=scl[:, 3:4], scalar2=None, op0=ALU.mult)
                    p.w(v, "sDV", p.inc(i0, "sDV"))
                    i1 = v.tensor_scalar(out=oslot2[fs][:, :], in0=iti[gi % 2][:, :],
                                         scalar1=1.0, scalar2=None, op0=ALU.mult)
                    p.w(v, "sDV", p.inc(i1, "sDV"))
                for m in range(DC):
                    k = tt * DC + m
                    p.w(v, "sAC", b_ac + k + 1)
                    ins = v.tensor_add(out=x_res[m][:, tt * TT:(tt + 1) * TT],
                                       in0=tmpb[:, (k % 2) * TT:(k % 2 + 1) * TT],
                                       in1=x_res[m][:, tt * TT:(tt + 1) * TT])
                    p.inc(ins, "sDV")

        @block.tensor
        def _(t):
            od, pm = ctx["onesd"], ctx["psm"]
            for tt in range(NT):
                for kk in range(F1C):
                    p.w(t, "sDV", b_dv + DVT * tt + 2 * kk + 1)
                    t.matmul(pm[:, :], od[:, :], od[:, :], start=True, stop=True)
                for m in range(DC):
                    k = tt * DC + m
                    p.w(t, wsem[k % 4], b_sw[k % 4] + 16 * (k // 4 + 1))
                    p.w(t, "sDV", b_dv + DVT * tt + 2 * F1C)
                    if k >= 4:
                        p.w(t, "sDV", b_dv + DVT * ((k - 4) // DC) + 2 * F1C + (k - 4) % DC + 1)
                    for fs in range(F1C):
                        ins = t.matmul(ps[m % 4][:, :], wslot[k % 4][:, fs * 128:(fs + 1) * 128],
                                       oslot2[fs][:, :], start=(fs == 0), stop=(fs == F1C - 1))
                        if fs == F1C - 1:
                            p.inc(ins, "sPE")

        @block.scalar
        def _(s):
            for tt in range(NT):
                for m in range(DC):
                    k = tt * DC + m
                    p.w(s, "sPE", b_pe + k + 1)
                    if k >= 2:
                        p.w(s, "sDV", b_dv + DVT * ((k - 2) // DC) + 2 * F1C + (k - 2) % DC + 1)
                    ins = s.activation(out=tmpb[:, (k % 2) * TT:(k % 2 + 1) * TT],
                                       in_=ps[m % 4][:, :],
                                       func=AF.Identity, scale=tmpa[:, 40 + m:41 + m],
                                       bias=b2q[:, m:m + 1])
                    p.inc(ins, "sAC")


def _attn_l1(ctx):
    p, nc = ctx["p"], ctx["nc"]
    x_res, xln, abuf, acc2 = ctx["x_res"], ctx["xln"], ctx["abuf"], ctx["acc2"]
    wst, sqt, ps, psm, pst = ctx["wst"], ctx["sqt"], ctx["ps"], ctx["psm"], ctx["pst"]
    tmpb, scl, zex, zexb = ctx["tmpb"], ctx["scl"], ctx["zex"], ctx["zexb"]
    battn, bkbc, bvbc, trimask, ident, ones1, constt = (
        ctx["battn"], ctx["bkbc"], ctx["bvbc"], ctx["trimask"], ctx["ident"], ctx["ones1"], ctx["constt"])
    d_wq, d_wk, d_wv, d_wo1 = ctx["d_wq"], ctx["d_wk"], ctx["d_wv"], ctx["d_wo1"]

    _layernorm(ctx)

    S_OF, Q_OF, K_OF, KT_OF, VT_OF, AT_OF, OF_OF = 0, 8192, 9216, 10240, 11264, 12304, 12432
    T1_OF, T2_OF, T3_OF, OT_OF = 13456, 13968, 14480, 14992
    WQ_OF, WK_OF, WO_OF = 0, 8192, 16384
    NCHW = 4
    sp_of = ctx["sp_o"]

    with nc.Block() as block:
        @block.gpsimd
        def _(g):
            for ds in range(DC):
                p.inc(g.dma_start(out=abuf[:, WQ_OF + ds * 1024:WQ_OF + (ds + 1) * 1024],
                                  in_=AP(d_wq, ds * 128, [[1024, 128], [131072, 8], [1, 128]])), "sG", 16)
                p.inc(g.dma_start(out=abuf[:, WK_OF + ds * 1024:WK_OF + (ds + 1) * 1024],
                                  in_=AP(d_wk, ds * 128, [[1024, 128], [131072, 8], [1, 128]])), "sG", 16)
            for c in range(DC):
                p.inc(g.dma_start(out=wst[c // 4][:, (c % 4) * 1024:(c % 4 + 1) * 1024],
                                  in_=d_wv[c * 128:(c + 1) * 128, :]), "sG", 16)
            p.w(g, "sG")

        @block.vector
        def _(v):
            for c in range(DC):
                v.memset(acc2[:, S_OF + c * 1024:S_OF + (c + 1) * 1024], 0.0)
            v.memset(zex[:, :], 0.0)
            i = v.memset(zexb[:, :], 0.0)
            p.w(v, "sDV", p.inc(i, "sDV"))

    qb = lambda ds: abuf[:, WQ_OF + ds * 1024:WQ_OF + (ds + 1) * 1024]
    kb = lambda ds: abuf[:, WK_OF + ds * 1024:WK_OF + (ds + 1) * 1024]
    Sb = lambda ds, h: acc2[:, S_OF + ds * 1024 + h * TT:S_OF + ds * 1024 + (h + 1) * TT]
    Qf = lambda ds: acc2[:, Q_OF + ds * 128:Q_OF + (ds + 1) * 128]
    Kf = lambda ds: acc2[:, K_OF + ds * 128:K_OF + (ds + 1) * 128]
    Ktm = lambda ds: acc2[:, KT_OF + ds * 128:KT_OF + (ds + 1) * 128]
    Ktm_h = lambda h: acc2[:, KT_OF + h * TT:KT_OF + (h + 1) * TT]
    Vtm_h = lambda h: acc2[:, VT_OF + h * TT:VT_OF + (h + 1) * TT]
    ATt = acc2[:, AT_OF:AT_OF + 128]
    Ofm = lambda ds: acc2[:, OF_OF + ds * 128:OF_OF + (ds + 1) * 128]
    Tt1 = acc2[:, T1_OF:T1_OF + 512]
    Tt2 = acc2[:, T2_OF:T2_OF + 512]
    Tt3 = acc2[:, T3_OF:T3_OF + 512]
    Otm = lambda h: acc2[:, OT_OF + h * TT:OT_OF + (h + 1) * TT]

    PE_N, DV_N, AC_N = 56, 117, 18
    B0 = dict(pe=p.cnt["sPE"], dv=p.cnt["sDV"], ac=p.cnt["sAC"])
    b0g = [p.cnt["sGa"], p.cnt["sGb"]]

    def base(n):
        return dict(pe=B0["pe"] + PE_N * n, dv=B0["dv"] + DV_N * n, ac=B0["ac"] + AC_N * n)

    with nc.Block() as block:
        @block.tensor
        def _(t):
            p.fence(t)
            for n in range(NCH):
                tok = slice(n * 128, (n + 1) * 128)
                B = base(n)
                for qk in range(2):
                    for ds in range(DC):
                        k = qk * DC + ds
                        if k >= 2:
                            p.w(t, "sDV", B["dv"] + 4 * (k - 2) + 2)
                        elif n > 0:
                            p.w(t, "sDV", base(n - 1)["dv"] + DV_N)
                        sl = ctx["psW"][:, 0:128] if k % 2 == 0 else ps[0][:, 0:128]
                        for c in range(DC):
                            ins = t.matmul(sl, (qb(ds) if qk == 0 else kb(ds))[:, c * 128:(c + 1) * 128],
                                           xln[c][:, tok], start=(c == 0), stop=(c == DC - 1))
                            if c == DC - 1:
                                p.inc(ins, "sPE")
                for kv in range(2):
                    for h in range(2):
                        k = kv * 2 + h
                        bank = ps[2 + (k % 2)]
                        if k >= 2:
                            p.w(t, "sDV", B["dv"] + 64 + (4 if k == 2 else 7))
                        elif n > 0:
                            p.w(t, "sDV", base(n - 1)["dv"] + DV_N)
                        for c in range(DC):
                            if kv == 0:
                                rhs = AP(ctx["abuf"], WK_OF + c * 128 + h * 4 * 1024,
                                         [[20736, 128], [1024, 4], [1, 128]])
                            else:
                                rhs = AP(wst[0] if (c < 4) else wst[1],
                                         (c % 4) * 1024 + h * 512, [[4096, 128], [1, 512]])
                            ins = t.matmul(bank[:, :], xln[c][:, tok], rhs,
                                           start=(c == 0), stop=(c == DC - 1))
                            if c == DC - 1:
                                p.inc(ins, "sPE")
                p.w(t, "sDV", B["dv"] + 64)
                for ds in range(DC):
                    ins = t.matmul(psm[:, :], Kf(ds), Qf(ds), start=(ds == 0), stop=(ds == DC - 1))
                    if ds == DC - 1:
                        p.inc(ins, "sPE")
                p.w(t, "sDV", B["dv"] + 73)
                for h in range(2):
                    for ds in range(DC):
                        ins = t.matmul(ps[h][:, :], Qf(ds), Sb(ds, h),
                                       start=(ds == 0), stop=False)
                    ins = t.matmul(ps[h][:, :], ATt, Vtm_h(h), start=False, stop=True)
                    p.inc(ins, "sPE")
                for ds in range(DC):
                    ins = t.matmul(psm[:, 0:1], Qf(ds), zexb[:, ds:ds + 1],
                                   start=(ds == 0), stop=False)
                ins = t.matmul(psm[:, 0:1], ATt, ones1[:, 0:1], start=False, stop=True)
                p.inc(ins, "sPE")
                for ds in range(DC):
                    p.w(t, "sDV", B["dv"] + 76 + (1 if ds >= 4 else 0))
                    if ds >= 1:
                        p.w(t, "sDV", B["dv"] + 77 + ds)
                    ins = t.transpose(pst[:, :], acc2[:, OT_OF + ds * 128:OT_OF + (ds + 1) * 128],
                                      ident[:, :])
                    p.inc(ins, "sPE")
                p.w(t, "sDV", B["dv"] + 72)
                for ds in range(DC):
                    for h in range(2):
                        k = ds * 2 + h
                        if k >= 2:
                            p.w(t, "sDV", B["dv"] + 85 + 4 * ((k - 2) // 2) + (k - 2) % 2 + 1)
                        bank = ps[2 + (k % 2)]
                        ins = t.matmul(bank[:, :], Ktm(ds), Vtm_h(h), start=True, stop=True)
                        p.inc(ins, "sPE")
                    if ds >= 1:
                        p.w(t, "sDV", B["dv"] + 85 + 4 * ds - 1)
                    else:
                        p.w(t, "sDV", B["dv"] + 74)
                    ins = t.matmul(psm[:, 1:2], Ktm(ds), ones1[:, 0:1], start=True, stop=True)
                    p.inc(ins, "sPE")

        @block.vector
        def _(v):
            for n in range(NCH):
                B = base(n)
                for qk in range(2):
                    biascol = 0 if qk == 0 else DC
                    for ds in range(DC):
                        k = qk * DC + ds
                        sl = ctx["psW"][:, 0:128] if k % 2 == 0 else ps[0][:, 0:128]
                        bias_ap = battn[:, biascol + ds:biascol + ds + 1]
                        p.w(v, "sPE", B["pe"] + k + 1)
                        p.w(v, "sDV")
                        p.w(v, "sAC", B["ac"] + k)
                        i0 = v.tensor_scalar(out=Tt1[:, 0:128], in0=sl, scalar1=bias_ap,
                                             scalar2=0.0, op0=ALU.add, op1=ALU.min)
                        p.w(v, "sDV", p.inc(i0, "sDV"))
                        p.w(v, "sAC", B["ac"] + k + 1)
                        i1 = v.tensor_scalar(out=Tt3[:, 0:128], in0=sl, scalar1=bias_ap,
                                             scalar2=0.0, op0=ALU.add, op1=ALU.max)
                        p.w(v, "sDV", p.inc(i1, "sDV"))
                        i2 = v.tensor_tensor(out=(Qf(ds) if qk == 0 else Kf(ds)),
                                             in0=Tt3[:, 0:128], in1=Tt2[:, 0:128], op=ALU.add)
                        p.w(v, "sDV", p.inc(i2, "sDV"))
                        p.inc(v.engine_nop(), "sDV")
                for kv in range(2):
                    for h in range(2):
                        k = kv * 2 + h
                        bank = ps[2 + (k % 2)]
                        p.w(v, "sPE", B["pe"] + 16 + k + 1)
                        p.w(v, "sDV")
                        if kv == 0 and h == 0:
                            p.w(v, "sAC", B["ac"] + 16)
                        bb = bkbc if kv == 0 else bvbc
                        if kv == 0:
                            i0 = v.tensor_tensor(out=Tt1[:, :], in0=bank[:, :],
                                                 in1=bb[:, h * TT:(h + 1) * TT], op=ALU.add)
                            p.w(v, "sDV", p.inc(i0, "sDV"))
                            i1 = v.tensor_scalar(out=Tt2[:, :], in0=Tt1[:, :],
                                                 scalar1=0.0, scalar2=None, op0=ALU.min)
                            p.w(v, "sDV", p.inc(i1, "sDV"))
                            p.w(v, "sAC", B["ac"] + 16 + (1 if h == 0 else 2))
                            i2 = v.scalar_tensor_tensor(out=Ktm_h(h), in0=Tt1[:, :], scalar=0.0,
                                                        in1=Tt3[:, :], op0=ALU.max, op1=ALU.add)
                            p.w(v, "sDV", p.inc(i2, "sDV"))
                        else:
                            i0 = v.tensor_tensor(out=Vtm_h(h), in0=bank[:, :],
                                                 in1=bb[:, h * TT:(h + 1) * TT], op=ALU.add)
                            p.w(v, "sDV", p.inc(i0, "sDV"))
                p.w(v, "sPE", B["pe"] + 21)
                i = v.tensor_tensor(out=ATt, in0=psm[:, :], in1=trimask[:, :], op=ALU.mult)
                p.w(v, "sDV", p.inc(i, "sDV"))
                p.w(v, "sPE", B["pe"] + 24)
                i0 = v.tensor_scalar(out=scl[:, 12:13], in0=psm[:, 0:1],
                                     scalar1=1e-6, scalar2=None, op0=ALU.add)
                p.w(v, "sDV", p.inc(i0, "sDV"))
                i1 = v.reciprocal(out=scl[:, 13:14], in_=scl[:, 12:13])
                p.w(v, "sDV", p.inc(i1, "sDV"))
                for h in range(2):
                    i2 = v.tensor_scalar(out=Otm(h), in0=ps[h][:, :],
                                         scalar1=scl[:, 13:14], scalar2=None, op0=ALU.mult)
                    p.w(v, "sDV", p.inc(i2, "sDV"))
                for ds in range(DC):
                    if ds == 0 and n >= 1:
                        p.w(v, "sGa", b0g[0] + 16 * ((n + 1) // 2))
                        p.w(v, "sGb", b0g[1] + 16 * (n // 2))
                    p.w(v, "sPE", B["pe"] + 24 + ds + 1)
                    i = v.tensor_scalar(out=Ofm(ds), in0=pst[:, :], scalar1=1.0,
                                        scalar2=None, op0=ALU.mult)
                    p.inc(i, "sDV")
                for ds in range(DC):
                    for h in range(2):
                        k = ds * 2 + h
                        bank = ps[2 + (k % 2)]
                        p.w(v, "sPE", B["pe"] + 32 + 3 * ds + h + 1)
                        p.w(v, "sDV")
                        i = v.tensor_tensor(out=Sb(ds, h), in0=bank[:, :], in1=Sb(ds, h), op=ALU.add)
                        p.w(v, "sDV", p.inc(i, "sDV"))
                    p.w(v, "sPE", B["pe"] + 32 + 3 * ds + 3)
                    i2 = v.tensor_tensor(out=zex[:, ds:ds + 1], in0=psm[:, 1:2],
                                         in1=zex[:, ds:ds + 1], op=ALU.add)
                    p.w(v, "sDV", p.inc(i2, "sDV"))
                    i3 = v.tensor_scalar(out=zexb[:, ds:ds + 1], in0=zex[:, ds:ds + 1],
                                         scalar1=1.0, scalar2=None, op0=ALU.mult)
                    p.w(v, "sDV", p.inc(i3, "sDV"))

        @block.scalar
        def _(s):
            for n in range(NCH):
                B = base(n)
                for qk in range(2):
                    for ds in range(DC):
                        k = qk * DC + ds
                        p.w(s, "sDV", B["dv"] + 4 * k + 1)
                        ins = s.activation(out=Tt2[:, 0:128], in_=Tt1[:, 0:128], func=AF.Exp)
                        p.inc(ins, "sAC")
                for h in range(2):
                    p.w(s, "sDV", B["dv"] + 64 + 3 * h + 2)
                    ins = s.activation(out=Tt3[:, :], in_=Tt2[:, :], func=AF.Exp)
                    p.inc(ins, "sAC")

        @block.gpsimd
        def _(g):
            for n in range(NCH):
                B = base(n)
                p.w(g, "sDV", B["dv"] + 85)
                p.inc(g.dma_start(out=AP(sp_of, n * 128, [[T, 128], [128 * T, 8], [1, 128]]),
                                  in_=acc2[:, OF_OF:OF_OF + 1024]),
                      "sGa" if n % 2 == 0 else "sGb", 16)
    assert p.cnt["sPE"] == B0["pe"] + PE_N * NCH, (p.cnt["sPE"], B0["pe"] + PE_N * NCH)
    assert p.cnt["sDV"] == B0["dv"] + DV_N * NCH, (p.cnt["sDV"], B0["dv"] + DV_N * NCH)
    assert p.cnt["sAC"] == B0["ac"] + AC_N * NCH

    with nc.Block() as block:
        b_snap = p.snap()
        b_sp = [p.cnt["sSa"], p.cnt["sSb"]]
        b_gp = [p.cnt["sGa"], p.cnt["sGb"]]
        b_pe = p.cnt["sPE"]
        b_ac = p.cnt["sAC"]
        b_dv = p.cnt["sDV"]

        @block.gpsimd
        def _(g):
            p.fence(g, b_snap)
            for m in range(DC):
                if m >= 2:
                    p.w(g, "sPE", b_pe + (m - 1) * NT)
                p.inc(g.dma_start(out=wst[m % 2][:, 0:1024],
                                  in_=AP(d_wo1, m * 128, [[1024, 128], [131072, 8], [1, 128]])),
                      "sGa" if m % 2 == 0 else "sGb", 16)

        @block.sync
        def _(sync):
            p.fence(sync, b_snap)
            for c in range(DC):
                p.inc(sync.dma_start(out=acc2[:, c * T:(c + 1) * T],
                                     in_=sp_of[c * 128:(c + 1) * 128, :]), "sS", 16)
            p.w(sync, "sS")

        @block.tensor
        def _(t):
            p.w(t, "sS")
            for m in range(DC):
                p.w(t, "sGa" if m % 2 == 0 else "sGb", b_gp[m % 2] + 16 * (m // 2 + 1))
                for tt in range(NT):
                    k = m * NT + tt
                    if k >= 4:
                        p.w(t, "sAC", b_ac + k - 3)
                    for ec in range(DC):
                        ins = t.matmul(ps[k % 4][:, :], wst[m % 2][:, ec * 128:(ec + 1) * 128],
                                       acc2[:, ec * T + tt * TT:ec * T + (tt + 1) * TT],
                                       start=(ec == 0), stop=(ec == DC - 1))
                        if ec == DC - 1:
                            p.inc(ins, "sPE")

        @block.scalar
        def _(s):
            for m in range(DC):
                for tt in range(NT):
                    k = m * NT + tt
                    p.w(s, "sPE", b_pe + k + 1)
                    if k >= 2:
                        p.w(s, "sDV", b_dv + k - 1)
                    ins = s.activation(out=sqt[k % 2][:, :], in_=ps[k % 4][:, :],
                                       func=AF.Identity, bias=battn[:, 3 * DC + m:3 * DC + m + 1],
                                       scale=1.0)
                    p.inc(ins, "sAC")

        @block.vector
        def _(v):
            for m in range(DC):
                for tt in range(NT):
                    k = m * NT + tt
                    p.w(v, "sAC", b_ac + k + 1)
                    ins = v.tensor_add(out=x_res[m][:, tt * TT:(tt + 1) * TT],
                                       in0=sqt[k % 2][:, :],
                                       in1=x_res[m][:, tt * TT:(tt + 1) * TT])
                    p.inc(ins, "sDV")



def _layout(inputs):
    x = np.asarray(inputs["L0_attn_x"])
    k0 = np.asarray(inputs["L0_attn_kernel"])
    wo0 = np.asarray(inputs["L0_attn_WO"])[0]
    base = {
        "k0_0": np.ascontiguousarray(k0[0].transpose(1, 0, 2).reshape(D, D)),
        "k0_1": np.ascontiguousarray(k0[1].transpose(1, 0, 2).reshape(D, D)),
        "k0_2": np.ascontiguousarray(k0[2].transpose(1, 0, 2).reshape(D, D)),
        "wo0": np.ascontiguousarray(wo0.reshape(DH, NHEAD, D).transpose(1, 0, 2).reshape(D, D)),
        "w_gate": np.asarray(inputs["L0_ffn_w_gate"]),
        "w_up": np.asarray(inputs["L0_ffn_w_up"]),
        "w_down": np.asarray(inputs["L0_ffn_w_down"]),
        "wqT": np.ascontiguousarray(np.asarray(inputs["L1_attn_Wq"]).T),
        "wkT": np.ascontiguousarray(np.asarray(inputs["L1_attn_Wk"]).T),
        "wvT": np.ascontiguousarray(np.asarray(inputs["L1_attn_Wv"]).T),
        "woT": np.ascontiguousarray(np.asarray(inputs["L1_attn_Wo"]).T),
        "battn4": np.ascontiguousarray(np.stack([
            np.asarray(inputs["L1_attn_bq"]), np.asarray(inputs["L1_attn_bk"]),
            np.asarray(inputs["L1_attn_bv"]), np.asarray(inputs["L1_attn_bo"])])),
        "w1": np.asarray(inputs["L1_ffn_w1"]),
        "b1": np.asarray(inputs["L1_ffn_b1"]),
        "w2": np.asarray(inputs["L1_ffn_w2"]),
        "b2": np.asarray(inputs["L1_ffn_b2"]),
    }
    maps = []
    for b in range(8):
        m = dict(base)
        m["x_t"] = np.ascontiguousarray(x[b].T)
        maps.append(m)
    return maps


def kernel(**inputs):
    maps = _layout(inputs)
    nc = build(upto=_UPTO, dbg=_DBG)
    res = run_bass_kernel_spmd(nc, maps, core_ids=list(range(8)))
    out = np.stack([np.ascontiguousarray(r["out_t"].T) for r in res.results])
    return out.astype(np.float32)
